# version 10
# baseline (speedup 1.0000x reference)
import numpy as np
import concourse.bass as bass
import concourse.mybir as mybir
from concourse.bass_utils import run_bass_kernel_spmd

F32 = mybir.dt.float32
BF16 = mybir.dt.bfloat16
AF = mybir.ActivationFunctionType
ALU = mybir.AluOpType
AX = mybir.AxisListType

S, D, H, HD, WIN = 4096, 1024, 16, 64, 512
TILE, NT, SC = 1024, 4, 512
NCH = 8
E8, RR, FF = 8, 256, 4096
SCALE = 0.125
CEXP = -8.0

ENGINES = ("tensor", "vector", "scalar", "sync", "gpsimd")


class Reg:
    __slots__ = ("writers", "readers")

    def __init__(self):
        self.writers = {}
        self.readers = {}

    def wrote(self, gate):
        s, v = gate
        self.writers = {s: v}
        self.readers = {}

    def read(self, gate):
        s, v = gate
        if self.readers.get(s, 0) < v:
            self.readers[s] = v


class Prog:
    def __init__(self):
        self.q = {e: [] for e in ENGINES}
        self.cnt = {}
        self.guar = {e: {} for e in ENGINES}
        self.pend = {e: [] for e in ENGINES}

    def sem(self, name):
        self.cnt[name] = 0
        return name

    def _wait(self, eng, sname, val):
        if val <= 0:
            return
        if self.guar[eng].get(sname, 0) >= val:
            return
        self.guar[eng][sname] = val
        self.q[eng].append(("w", sname, val))

    def emit(self, eng, fn, reads=(), writes=(), inc=None, k=1):
        need = {}
        for r in reads:
            for s, v in r.writers.items():
                if need.get(s, 0) < v:
                    need[s] = v
        for r in writes:
            for d in (r.writers, r.readers):
                for s, v in d.items():
                    if need.get(s, 0) < v:
                        need[s] = v
        if inc is not None and inc[0] == "d":
            need.pop(inc, None)
        for s, v in need.items():
            self._wait(eng, s, v)
        if inc is None:
            self.q[eng].append(("op", fn, None, 0))
            self.pend[eng].append((tuple(reads), tuple(writes)))
            return None
        self.cnt[inc] += k
        gate = (inc, self.cnt[inc])
        if eng == "tensor" and inc == "T":
            self.guar[eng][inc] = self.cnt[inc]
        self.q[eng].append(("op", fn, inc, k))
        for rr, ww in self.pend[eng]:
            for r in rr:
                r.read(gate)
            for w in ww:
                w.wrote(gate)
        self.pend[eng].clear()
        for r in reads:
            r.read(gate)
        for r in writes:
            r.wrote(gate)
        return gate

    def raw_wait(self, eng, gatedict):
        for s, v in gatedict.items():
            self._wait(eng, s, v)


def alias_edge(dsts, srcs):
    merged = {}
    for o in srcs:
        for d in (o.writers, o.readers):
            for s, v in d.items():
                if merged.get(s, 0) < v:
                    merged[s] = v
    for o in srcs:
        o.writers = {}
        o.readers = {}
    for r in dsts:
        r.writers = {}
        r.readers = dict(merged)


def interleave(gens):
    active = [[g, w] for g, w in gens]
    while active:
        for item in list(active):
            g, w = item
            for _ in range(w):
                try:
                    next(g)
                except StopIteration:
                    active.remove(item)
                    break


def build_nc():
    nc = bass.Bass("TRN2", target_bir_lowering=False, debug=False)
    P = Prog()
    for s in ("T", "V", "S", "G"):
        P.sem(s)
    DMA_SEMS = ("dxy", "dsl0", "dsl1", "dpf0", "dpf1", "dgw1", "dw1", "doutw",
                "dw2a", "dw2b", "dl2a", "dl2b", "dmisc", "dbias",
                "dout0", "dout1", "dsb", "dsw2", "dsw2b", "dsl2s", "dsl2t")
    for s in DMA_SEMS:
        P.sem(s)

    dt_in = lambda n, sh, dt=F32: nc.dram_tensor(n, sh, dt, kind="ExternalInput").ap()
    x_tm = dt_in("x_tm", [S, D])
    x_fm = dt_in("x_fm", [D, S])
    pos_tm = dt_in("pos_tm", [S, D])
    pos_fm = dt_in("pos_fm", [D, S])
    gmem = dt_in("gmem", [64, D])
    convw = dt_in("convw", [4097, D])
    gate_w = dt_in("gate_w", [2048, D])
    out_w = dt_in("out_w", [D, D])
    w1f = dt_in("w1f", [D, 2048])
    w2f = dt_in("w2f", [2048, FF])
    lin2_w = dt_in("lin2_w", [FF, D])
    gate_b = dt_in("gate_b", [128, 8])
    out_b = dt_in("out_b", [128, 8])
    lin2_b = dt_in("lin2_b", [128, 8])
    router_w = dt_in("router_w", [128, 64])
    router_b = dt_in("router_b", [8, 1])
    out_fm = nc.dram_tensor("out_fm", [D, S], F32, kind="ExternalOutput").ap()
    dt_sc = lambda n, sh: nc.dram_tensor(n, sh, BF16, kind="Internal").ap()
    gw_bf = dt_sc("gw_bf", [2048, D])
    ow_bf = dt_sc("ow_bf", [D, D])
    w1_bf = dt_sc("w1_bf", [D, 2048])
    w2_bf = dt_sc("w2_bf", [2048, FF])
    l2_bf = dt_sc("l2_bf", [FF, D])

    import contextlib
    stack = contextlib.ExitStack()
    sb = lambda n, sh, dt: stack.enter_context(nc.sbuf_tensor(n, sh, dt)).ap()
    ps = lambda n, sh, dt: stack.enter_context(nc.psum_tensor(n, sh, dt)).ap()

    xy = sb("xy", [128, 8192], BF16)
    ynbf = sb("ynbf", [128, 8192], BF16)
    xpef = sb("xpef", [128, 4096], BF16)
    xpet = sb("xpet", [128, 4096], BF16)
    ah = sb("ah", [128, 16384], BF16)
    tbuf = sb("tbuf", [128, 8192], BF16)
    mixb = tbuf[:, 0:4096]
    gw1a = sb("gw1a", [128, 16384], BF16)
    outwa = sb("outwa", [128, 8192], BF16)
    w2a = sb("w2a", [128, 8192], BF16)
    l2a = sb("l2a", [128, 4096], BF16)
    l2acc = sb("l2acc", [128, 4096], BF16)
    outst = sb("outst", [128, 512], F32)
    gt_tm = sb("gt_tm", [128, 1024], BF16)
    gt_fm = sb("gt_fm", [128, 1024], BF16)
    ebf = sb("ebf", [128, 512], BF16)
    cfm = sb("cfm", [128, 512], BF16)
    ctm = sb("ctm", [128, 32], BF16)
    csb = sb("csb", [128, 512], BF16)
    mwsb = sb("mwsb", [128, 1024], BF16)
    ztm = sb("ztm", [128, 16], F32)
    rztm = sb("rztm", [128, 16], BF16)
    scr4 = sb("scr4", [128, 16], F32)
    scrf = sb("scrf", [128, 576], F32)
    statv = sb("statv", [1, 1536], F32)
    ones_b = sb("ones_b", [128, 128], BF16)
    id_b = sb("id_b", [128, 128], BF16)
    basis = sb("basis", [8, 1024], BF16)
    neg8 = sb("neg8", [128, 1], F32)
    epsc = sb("epsc", [128, 1], F32)
    gb_s = sb("gb_s", [128, 8], F32)
    ob_s = sb("ob_s", [128, 8], F32)
    lb_s = sb("lb_s", [128, 8], F32)
    rb_s = sb("rb_s", [8, 1], F32)
    rw_s = sb("rw_s", [128, 64], BF16)
    cbrow = None
    iot = sb("iot", [128, 1], F32)

    cbrow = l2acc[0:1, 0:1024]
    pb = [ps("pb%d" % i, [128, 512], F32) for i in range(8)]
    pbb = [p.bitcast(BF16) for p in pb]

    LOC, GLO, PL0, PG0, RZ0, HQ0 = 0, 4096, 8192, 10240, 11264, 12288

    R = lambda: Reg()
    Rxy, Rynbf, Rxpef, Rxpet = R(), R(), R(), R()
    Rloc, Rglo, Rmix = R(), R(), R()
    Rpl = [R() for _ in range(4)]
    Rpg = [R() for _ in range(2)]
    Rrz = [R() for _ in range(2)]
    Rhq = [R() for _ in range(8)]
    Rtb = R()
    Rgw1, Rw1, Routw = R(), R(), R()
    Rw2 = [R(), R()]
    Rconv = [R(), R()]
    Rl2 = [R(), R()]
    Rl2acc = R()
    Rout = [R(), R()]
    Rgt_tm, Rgt_fm = R(), R()
    Rgb, Rob, Rlb, Rrb, Rrw, Rcbrow = R(), R(), R(), R(), R(), R()
    Rsgw, Rsow, Rsw1, Rsw2, Rsl2 = R(), R(), R(), R(), R()
    Rl2s_part = [R() for _ in range(4)]
    Rebf, Rcfm, Rctm, Rcsb = R(), R(), R(), R()
    Rmwsb = [R(), R()]
    Rzt = [R(), R()]
    Rrzt = [R(), R()]
    Rscr4, Rscrf, Rstat = R(), R(), R()
    Rb = [R() for _ in range(8)]
    Rconst = R()

    def gp(fn, **kw):
        return P.emit("gpsimd", fn, **kw)

    def vx(fn, **kw):
        return P.emit("vector", fn, **kw)

    def sx(fn, **kw):
        return P.emit("scalar", fn, **kw)

    def tx(fn, **kw):
        return P.emit("tensor", fn, **kw)

    def dmaw(sem, fn, **kw):
        return P.emit("gpsimd", fn, inc=sem, k=16, **kw)

    def dmax(sem, fn, **kw):
        return P.emit("sync", fn, inc=sem, k=16, **kw)

    gp(lambda E: E.memset(ones_b[:], 1.0), writes=[Rconst], inc="G")
    gp(lambda E: E.memset(neg8[:], CEXP), writes=[Rconst], inc="G")
    gp(lambda E: E.memset(epsc[:], 1e-5), writes=[Rconst], inc="G")
    gp(lambda E: E.iota(id_b[:], pattern=[[1, 128]], channel_multiplier=0,
                        allow_small_or_imprecise_dtypes=True),
       writes=[Rconst], inc="G")
    gp(lambda E: E.iota(iot[:], pattern=[[0, 1]], channel_multiplier=1,
                        allow_small_or_imprecise_dtypes=True),
       writes=[Rconst], inc="G")
    vx(lambda E: E.tensor_scalar(id_b[:], id_b[:], iot[:], None, ALU.is_equal),
       reads=[Rconst], writes=[Rconst], inc="V")
    gp(lambda E: E.iota(basis[:], pattern=[[1, 8], [0, 128]], channel_multiplier=0,
                        allow_small_or_imprecise_dtypes=True),
       writes=[Rconst], inc="G")
    vx(lambda E: E.tensor_scalar(basis[:], basis[:], iot[0:8, :], None, ALU.is_equal),
       reads=[Rconst], writes=[Rconst], inc="V")

    dmax("dbias", lambda E: E.dma_start(out=gb_s[:], in_=gate_b), writes=[Rgb])
    dmax("dbias", lambda E: E.dma_start(out=ob_s[:], in_=out_b), writes=[Rob])
    dmax("dbias", lambda E: E.dma_start(out=lb_s[:], in_=lin2_b), writes=[Rlb])
    dmax("dbias", lambda E: E.dma_start(out=rb_s[:], in_=router_b), writes=[Rrb])
    dmaw("dmisc", lambda E: E.dma_start(out=rw_s[:], in_=router_w), writes=[Rrw])
    dmaw("dmisc", lambda E: E.dma_start(out=gt_tm[64:128, :], in_=gmem),
         writes=[Rgt_tm])
    dmaw("dmisc", lambda E: E.dma_start(out=cbrow[:], in_=convw[4096:4097, :]),
         writes=[Rcbrow])
    for r in (Rgb, Rob, Rlb, Rrb):
        r.writers = {"dbias": P.cnt["dbias"]}
    for r in (Rrw, Rgt_tm, Rcbrow):
        r.writers = {"dmisc": P.cnt["dmisc"]}

    conv_src = convw[0:4096, :].rearrange("(j p) c -> p j c", p=128)

    def conv_group_load(g):
        slot = g % 2
        dst = w2a[:, slot * 4096:(slot + 1) * 4096].rearrange(
            "p (j c) -> p j c", c=1024)
        dmaw("dw2a" if slot == 0 else "dw2b",
             lambda E, g=g, d=dst: E.dma_start(
                 out=d, in_=conv_src[:, 4 * g:4 * g + 4, :]),
             writes=[Rconv[slot]])

    conv_group_load(0)
    conv_group_load(1)

    gw_src = gate_w.rearrange("(k p) c -> p k c", p=128)
    ow_src = out_w.rearrange("(k p) c -> p k c", p=128)
    w1_src = w1f.rearrange("(k p) c -> p k c", p=128)
    l2_src = lin2_w.rearrange("(k p) c -> p k c", p=128)
    w2_src = w2f.rearrange("(k p) (e c) -> p e k c", p=128, c=256)
    gwb_src = gw_bf.rearrange("(k p) c -> p k c", p=128)
    owb_src = ow_bf.rearrange("(k p) c -> p k c", p=128)
    w1b_src = w1_bf.rearrange("(k p) c -> p k c", p=128)
    l2b_src = l2_bf.rearrange("(k p) c -> p k c", p=128)
    w2b_src = w2_bf.rearrange("(k p) (e c) -> p e k c", p=128, c=256)

    def wb_store(sbuf_view, dst_view, slab_reg, scr_reg, sem="dsb"):
        P.emit("sync", lambda E, s=sbuf_view, d=dst_view: E.dma_start(out=d, in_=s),
               reads=[slab_reg], writes=[scr_reg], inc=sem, k=16)

    dmaw("dgw1", lambda E: E.dma_start(out=gw1a.rearrange("p (k c) -> p k c", c=1024),
                                       in_=gw_src), writes=[Rgw1])
    wb_store(gw1a.rearrange("p (k c) -> p k c", c=1024), gwb_src, Rgw1, Rsgw)
    dmaw("doutw", lambda E: E.dma_start(out=outwa.rearrange("p (k c) -> p k c", c=1024),
                                        in_=ow_src), writes=[Routw])
    wb_store(outwa.rearrange("p (k c) -> p k c", c=1024), owb_src, Routw, Rsow)
    for r in (Rsgw, Rsow):
        r.writers = {"dsb": P.cnt["dsb"]}
    Rgw1.readers["dsb"] = P.cnt["dsb"]
    Routw.readers["dsb"] = P.cnt["dsb"]

    xfm_g = x_fm.rearrange("(f p) s -> p f s", p=128)
    pfm_g = pos_fm.rearrange("(f p) s -> p f s", p=128)
    xtm_g = x_tm.rearrange("(g p) d -> p g d", p=128)
    ptm_g = pos_tm.rearrange("(g p) d -> p g d", p=128)

    def attn_gen(t, sc):
        tb0 = t * 1024
        scb = sc * 512
        pcnt = 0
        rot = [0]

        def nb():
            rot[0] ^= 1
            return rot[0]

        def pair_normalize(pslot, fc, wloc, rreg):
            yield
            for hpar in range(2):
                zb = nb()
                for kc in range(4):
                    rc = rztm[:, pslot * 8 + hpar * 4 + kc:
                              pslot * 8 + hpar * 4 + kc + 1]
                    out = pbb[zb][0:1, kc * 128:kc * 128 + 128]
                    tx(lambda E, i=rc, o=out, kc=kc: E.matmul(
                        o, i, id_b[:], start=(kc == 0), stop=(kc == 3),
                        is_transpose=True),
                       reads=[Rrzt[pslot], Rconst], writes=[Rb[zb]],
                       inc=("T" if kc == 3 else None))
                rzv = ah[0:1, RZ0 + hpar * 512: RZ0 + hpar * 512 + 512]
                vx(lambda E, o=rzv, b=zb: E.tensor_copy(o, pbb[b][0:1, 0:512]),
                   reads=[Rb[zb]], writes=[Rrz[hpar]], inc="V")
            yield
            cb = nb()
            rzA = ah[0:1, RZ0: RZ0 + 512]
            rzB = ah[0:1, RZ0 + 512: RZ0 + 1024]
            tx(lambda E, r=rzA, b=cb: E.matmul(pb[b][0:64, :], ones_b[0:1, 0:64], r,
                                               start=True, stop=True),
               reads=[Rrz[0], Rconst], writes=[Rb[cb]], inc=None)
            tx(lambda E, r=rzB, b=cb: E.matmul(pb[b][64:128, :], ones_b[0:1, 0:64], r,
                                               start=True, stop=True),
               reads=[Rrz[1], Rconst], writes=[Rb[cb]], inc="T")
            vx(lambda E, b=cb: E.tensor_copy(csb[:], pb[b][:]),
               reads=[Rb[cb]], writes=[Rcsb], inc="V")
            oA = ah[0:64, wloc + fc * 512: wloc + fc * 512 + 512]
            oB = ah[64:128, wloc + fc * 512: wloc + fc * 512 + 512]
            vx(lambda E, o=oA: E.tensor_mul(o, pb[2][0:64, :], csb[0:64, :]),
               reads=[Rb[2], Rcsb], writes=[rreg], inc="V")
            vx(lambda E, o=oB: E.tensor_mul(o, pb[3][64:128, :], csb[64:128, :]),
               reads=[Rb[3], Rcsb], writes=[rreg], inc="V")

        for pr in range(8):
            h0, h1 = 2 * pr, 2 * pr + 1
            fc = pr
            pslot = pr % 2
            for h, pbase, avb in ((h0, 0, 2), (h1, 64, 3)):
                for kc in range(4):
                    b = nb()
                    lhs = xpef[pbase:pbase + 64,
                               fc * 512 + kc * 128: fc * 512 + kc * 128 + 128]
                    rhs = xpef[pbase:pbase + 64, fc * 512: fc * 512 + 512]
                    tx(lambda E, l=lhs, r=rhs, b=b: E.matmul(
                        pb[b][:], l, r, start=True, stop=True),
                       reads=[Rxpef], writes=[Rb[b]], inc="T")
                    slot = pcnt % 4
                    pcnt += 1
                    pls = ah[:, PL0 + slot * 512: PL0 + slot * 512 + 512]
                    zc = ztm[:, pslot * 8 + (h % 2) * 4 + kc:
                             pslot * 8 + (h % 2) * 4 + kc + 1]
                    sx(lambda E, b=b, o=pls, z=zc: E.activation(
                        o, pb[b][:], AF.Exp, bias=neg8[:], scale=SCALE, accum_out=z),
                       reads=[Rb[b], Rconst], writes=[Rpl[slot], Rzt[pslot]],
                       inc="S")
                for kc in range(4):
                    slot = (pcnt - 4 + kc) % 4
                    pls = ah[:, PL0 + slot * 512: PL0 + slot * 512 + 512]
                    lhs = xpet[:, kc * 1024 + 64 * h: kc * 1024 + 64 * h + 64]
                    out = pb[avb][pbase:pbase + 64, :]
                    tx(lambda E, l=lhs, r=pls, o=out, kc=kc: E.matmul(
                        o, l, r, start=(kc == 0), stop=(kc == 3)),
                       reads=[Rpl[slot], Rxpet], writes=[Rb[avb]],
                       inc=("T" if kc == 3 else None))
                yield
            vx(lambda E, ps_=pslot: E.reciprocal(rztm[:, ps_ * 8: ps_ * 8 + 8],
                                                 ztm[:, ps_ * 8: ps_ * 8 + 8]),
               reads=[Rzt[pslot]], writes=[Rrzt[pslot]], inc="V")
            for _ in pair_normalize(pslot, fc, LOC, Rloc):
                yield
            yield

        if sc == 0:
            for fc2 in range(NCH):
                srcx = x_fm[fc2 * 128:(fc2 + 1) * 128, tb0:tb0 + 1024]
                dstx = xy[:, fc2 * 1024: fc2 * 1024 + 1024]
                dmaw("dxy", lambda E, s=srcx, d=dstx: E.dma_start(out=d, in_=s),
                     writes=[Rxy])

        if t == 0 and sc == 0:
            for j in range(32):
                kc, ic = j // 8, j % 8
                g, jj = j // 4, j % 4
                slot = g % 2
                if jj == 0 and g >= 2:
                    conv_group_load(g)
                lhs = xpef[:, ic * 512: ic * 512 + 256].rearrange(
                    "p (t f) -> p t f", f=4)[:, :, kc]
                for oh in range(2):
                    bank = 2 + oh
                    rhs = w2a[:, slot * 4096 + jj * 1024 + oh * 512:
                              slot * 4096 + jj * 1024 + oh * 512 + 512]
                    tx(lambda E, l=lhs, r=rhs, b=bank, j=j: E.matmul(
                        pb[b][0:64, :], l, r, start=(j == 0), stop=False),
                       reads=[Rxpef, Rconv[slot]], writes=[Rb[bank]],
                       inc=("T" if (j % 4) == 3 and oh == 1 else None))
                if j % 8 == 7:
                    yield
            for oh in range(2):
                bank = 2 + oh
                rhsb = cbrow[0:1, oh * 512: oh * 512 + 512]
                tx(lambda E, r=rhsb, b=bank: E.matmul(
                    pb[b][0:64, :], ones_b[0:1, 0:64], r, start=False, stop=True),
                   reads=[Rcbrow, Rconst], writes=[Rb[bank]], inc="T")
                vx(lambda E, b=bank, oh=oh: E.tensor_copy(
                    gt_tm[0:64, oh * 512: oh * 512 + 512], pb[b][0:64, :]),
                   reads=[Rb[bank]], writes=[Rgt_tm], inc="V")
            for fc2 in range(NCH):
                tin = gt_tm[:, fc2 * 128: fc2 * 128 + 128]
                tx(lambda E, i=tin: E.matmul(pbb[0][:, 0:128], i, id_b[:],
                                             start=True, stop=True,
                                             is_transpose=True),
                   reads=[Rgt_tm, Rconst], writes=[Rb[0]], inc="T")
                vx(lambda E, fc2=fc2: E.tensor_copy(
                    gt_fm[:, fc2 * 128: fc2 * 128 + 128], pbb[0][:, 0:128]),
                   reads=[Rb[0]], writes=[Rgt_fm], inc="V")
                if fc2 % 4 == 3:
                    yield

        for pr in range(8):
            h0, h1 = 2 * pr, 2 * pr + 1
            fc = pr
            pslot = pr % 2
            for h, pbase, avb, pgslot in ((h0, 0, 2, 0), (h1, 64, 3, 1)):
                b = nb()
                lhs = gt_fm[pbase:pbase + 64, fc * 128: fc * 128 + 128]
                rhs = xpef[pbase:pbase + 64, fc * 512: fc * 512 + 512]
                tx(lambda E, l=lhs, r=rhs, b=b: E.matmul(
                    pb[b][:], l, r, start=True, stop=True),
                   reads=[Rgt_fm, Rxpef], writes=[Rb[b]], inc="T")
                pgs = ah[:, PG0 + pgslot * 512: PG0 + pgslot * 512 + 512]
                sx(lambda E, b=b, o=pgs: E.activation(o, pb[b][:], AF.Exp,
                                                      bias=neg8[:], scale=SCALE),
                   reads=[Rb[b], Rconst], writes=[Rpg[pgslot]], inc="S")
                zb = nb()
                for tt in range(4):
                    zout = pb[zb][:, tt: tt + 1]
                    tx(lambda E, l=pgs, o=zout, tt=tt: E.matmul(
                        o, l[:, tt * 128:tt * 128 + 128], ones_b[:, 0:1],
                        start=(tt == 0), stop=(tt == 3)),
                       reads=[Rpg[pgslot], Rconst], writes=[Rb[zb]],
                       inc=("T" if tt == 3 else None))
                vx(lambda E, ps_=pslot, b=zb, hh_=(h % 2): E.reciprocal(
                    rztm[:, ps_ * 8 + hh_ * 4: ps_ * 8 + hh_ * 4 + 4],
                    pb[b][:, 0:4]),
                   reads=[Rb[zb]], writes=[Rrzt[pslot]], inc="V")
                lhs2 = gt_tm[:, h * 64: h * 64 + 64]
                out = pb[avb][pbase:pbase + 64, :]
                tx(lambda E, l=lhs2, r=pgs, o=out: E.matmul(
                    o, l, r, start=True, stop=True),
                   reads=[Rpg[pgslot], Rgt_tm], writes=[Rb[avb]], inc="T")
            for _ in pair_normalize(pslot, fc, GLO, Rglo):
                yield
            yield

    def ln_stage(sc_base, out_kind, t, bank_ids, scratch):
        bsy, bsq, bb0, bb1 = bank_ids
        srct, srcr = (xy, Rxy) if out_kind == "ln1" else (ynbf, Rynbf)
        for fc in range(NCH):
            xsl = srct[:, fc * 1024 + sc_base: fc * 1024 + sc_base + 512]
            pl, srg = scratch(fc)
            sx(lambda E, i=xsl, o=pl: E.activation(o, i, AF.Square),
               reads=[srcr], writes=[srg], inc="S")
            tx(lambda E, r=xsl, fc=fc: E.matmul(pb[bsy][0:1, :], ones_b[:, 0:1], r,
                                                start=(fc == 0), stop=(fc == 7)),
               reads=[srcr, Rconst], writes=[Rb[bsy]], inc="T")
            tx(lambda E, r=pl, fc=fc: E.matmul(pb[bsq][0:1, :], ones_b[:, 0:1], r,
                                               start=(fc == 0), stop=(fc == 7)),
               reads=[srg, Rconst], writes=[Rb[bsq]], inc="T")
            if fc % 4 == 3:
                yield
        mean = statv[0:1, 0:512]
        inv = statv[0:1, 512:1024]
        invb = statv.bitcast(BF16)[0:1, 2048:2560]
        minvb = statv.bitcast(BF16)[0:1, 2560:3072]
        vx(lambda E: E.tensor_scalar(mean, pb[bsy][0:1, :], 1.0 / 1024, None,
                                     ALU.mult),
           reads=[Rb[bsy]], writes=[Rstat], inc="V")
        vx(lambda E: E.tensor_mul(inv, mean, mean), reads=[Rstat], writes=[Rstat],
           inc="V")
        vx(lambda E: E.scalar_tensor_tensor(inv, pb[bsq][0:1, :], 1.0 / 1024, inv,
                                            ALU.mult, ALU.subtract),
           reads=[Rb[bsq], Rstat], writes=[Rstat], inc="V")
        sx(lambda E: E.activation(inv, inv, AF.Ln, bias=epsc[0:1, :]),
           reads=[Rstat, Rconst], writes=[Rstat], inc="S")
        sx(lambda E: E.activation(inv, inv, AF.Exp, scale=-0.5),
           reads=[Rstat], writes=[Rstat], inc="S")
        vx(lambda E: E.tensor_copy(invb, inv), reads=[Rstat], writes=[Rstat], inc="V")
        vx(lambda E: E.tensor_mul(mean, mean, inv), reads=[Rstat], writes=[Rstat],
           inc="V")
        vx(lambda E: E.tensor_copy(minvb, mean), reads=[Rstat], writes=[Rstat],
           inc="V")
        yield
        tx(lambda E: E.matmul(pb[bb0][:], ones_b[0:1, :], invb, start=True, stop=True),
           reads=[Rstat, Rconst], writes=[Rb[bb0]], inc="T")
        tx(lambda E: E.matmul(pb[bb1][:], ones_b[0:1, :], minvb, start=True,
                              stop=True),
           reads=[Rstat, Rconst], writes=[Rb[bb1]], inc="T")
        yield
        for fc in range(NCH):
            xsl = srct[:, fc * 1024 + sc_base: fc * 1024 + sc_base + 512]
            q1 = scrf[:, 0:512]
            vx(lambda E, i=xsl, o=q1: E.tensor_mul(o, i, pb[bb0][:]),
               reads=[srcr, Rb[bb0]], writes=[Rscrf], inc="V")
            if out_kind == "ln1":
                osl = ynbf[:, fc * 1024 + sc_base: fc * 1024 + sc_base + 512]
                vx(lambda E, i=q1, o=osl: E.tensor_sub(o, i, pb[bb1][:]),
                   reads=[Rscrf, Rb[bb1]], writes=[Rynbf], inc="V")
            else:
                slot = 0
                osl = outst[:, 0:512]
                vx(lambda E, i=q1, o=osl: E.tensor_sub(o, i, pb[bb1][:]),
                   reads=[Rscrf, Rb[bb1]], writes=[Rout[slot]], inc="V")
                dst = out_fm[fc * 128:(fc + 1) * 128,
                             t * 1024 + sc_base: t * 1024 + sc_base + 512]
                P.emit("sync", lambda E, s=osl, d=dst: E.dma_start(out=d, in_=s),
                       reads=[Rout[slot]], inc="dout%d" % slot, k=16)
            if fc % 2 == 1:
                yield

    def e_phase(t, sc):
        scb = sc * 512
        alias_edge([Rmix], [Rtb])
        for m in range(NCH):
            bank = m % 2
            for k in range(16):
                lhs = gw1a[:, k * 1024 + m * 128: k * 1024 + m * 128 + 128]
                base = LOC if k < 8 else GLO
                rr = Rloc if k < 8 else Rglo
                rhs = ah[:, base + (k % 8) * 512: base + (k % 8) * 512 + 512]
                tx(lambda E, l=lhs, r=rhs, b=bank, k=k: E.matmul(
                    pb[b][:], l, r, start=(k == 0), stop=(k == 15)),
                   reads=[Rgw1, rr], writes=[Rb[bank]],
                   inc=("T" if k == 15 else None))
            msl = mixb[:, m * 512: m * 512 + 512]
            sx(lambda E, b=bank, o=msl, m=m: E.activation(
                o, pb[b][:], AF.Sigmoid, bias=gb_s[:, m:m + 1]),
               reads=[Rb[bank], Rgb], writes=[Rmix], inc="S")
            yield
        for fc in range(NCH):
            lsl = ah[:, LOC + fc * 512: LOC + fc * 512 + 512]
            gsl = ah[:, GLO + fc * 512: GLO + fc * 512 + 512]
            msl = mixb[:, fc * 512: fc * 512 + 512]
            t1 = ah[:, PL0 + (fc % 4) * 512: PL0 + (fc % 4) * 512 + 512]
            vx(lambda E, a=lsl, b=gsl, o=t1: E.tensor_sub(o, a, b),
               reads=[Rloc, Rglo], writes=[Rpl[fc % 4]], inc="V")
            vx(lambda E, a=t1, b=msl, o=t1: E.tensor_mul(o, a, b),
               reads=[Rpl[fc % 4], Rmix], writes=[Rpl[fc % 4]], inc="V")
            vx(lambda E, a=t1, b=gsl, o=msl: E.tensor_add(o, a, b),
               reads=[Rpl[fc % 4], Rglo], writes=[Rmix], inc="V")
        yield
        for m in range(NCH):
            bank = 2 + (m % 2)
            for k in range(NCH):
                lhs = outwa[:, k * 1024 + m * 128: k * 1024 + m * 128 + 128]
                rhs = mixb[:, k * 512: k * 512 + 512]
                tx(lambda E, l=lhs, r=rhs, b=bank, k=k: E.matmul(
                    pb[b][:], l, r, start=(k == 0), stop=(k == 7)),
                   reads=[Routw, Rmix], writes=[Rb[bank]],
                   inc=("T" if k == 7 else None))
            xsl = xy[:, m * 1024 + scb: m * 1024 + scb + 512]
            vx(lambda E, b=bank, o=xsl, m=m: E.scalar_tensor_tensor(
                o, pb[b][:], ob_s[:, m:m + 1], o, ALU.add, ALU.add),
               reads=[Rb[bank], Rob, Rxy], writes=[Rxy], inc="V")
            if m % 2 == 1:
                yield
        for _ in ln_stage(scb, "ln1", t, (0, 1, 2, 3),
                          lambda fc: (ah[:, PL0 + (fc % 2) * 512:
                                         PL0 + (fc % 2) * 512 + 512],
                                      Rpl[fc % 2])):
            yield

    def ffn_gen(t, sc):
        scb = sc * 512
        first = (t == 0 and sc == 0)
        alias_edge([Rtb], [Rmix])
        if first:
            alias_edge([Rl2acc], [Rcbrow])
        if first:
            dmaw("dw1", lambda E: E.dma_start(
                out=gw1a.rearrange("p (k c) -> p k c", c=2048), in_=w1_src),
                 writes=[Rgw1])
            wb_store(gw1a.rearrange("p (k c) -> p k c", c=2048), w1b_src,
                     Rgw1, Rsw1, "dsb")
            Rsw1.writers = {"dsb": P.cnt["dsb"]}
        else:
            P.emit("gpsimd", lambda E: E.dma_start(
                out=gw1a.rearrange("p (k c) -> p k c", c=2048), in_=w1b_src),
                reads=[Rsw1], writes=[Rgw1], inc="dw1", k=16)
        for fc in range(NCH):
            lhs = rw_s[:, fc * 8: fc * 8 + 8]
            rhs = ynbf[:, fc * 1024 + scb: fc * 1024 + scb + 512]
            tx(lambda E, l=lhs, r=rhs, fc=fc: E.matmul(
                pb[4][0:8, :], l, r, start=(fc == 0), stop=(fc == 7)),
               reads=[Rrw, Rynbf], writes=[Rb[4]],
               inc=("T" if fc == 7 else None))
        sx(lambda E: E.activation(ebf[0:8, :], pb[4][0:8, :], AF.Exp, bias=rb_s[:]),
           reads=[Rb[4], Rrb], writes=[Rebf], inc="S")
        for u in range(4):
            tin = ebf[0:8, u * 128: u * 128 + 128]
            out = pbb[5][:, u * 8: u * 8 + 8]
            tx(lambda E, i=tin, o=out, u=u: E.matmul(
                o, i, id_b[0:8, 0:8], start=(u == 0), stop=(u == 3),
                is_transpose=True),
               reads=[Rebf, Rconst], writes=[Rb[5]],
               inc=("T" if u == 3 else None))
        yield
        et = pbb[5][:, 0:32].rearrange("p (a b) -> p a b", b=8)
        etf = pbb[5][:, 0:32]
        zt = scr4[:, 8:12]
        rzt = scr4[:, 12:16]
        m1 = scr4[:, 0:4]
        m2 = scr4[:, 4:8]
        eq = scrf[:, 512:544]
        dd = scrf[:, 544:576]
        vx(lambda E: E.reduce_sum(zt, et, axis=AX.X), reads=[Rb[5]], writes=[Rscr4],
           inc="V")
        vx(lambda E: E.reciprocal(rzt, zt), reads=[Rscr4], writes=[Rscr4], inc="V")
        vx(lambda E: E.reduce_max(m1, et, axis=AX.X), reads=[Rb[5]], writes=[Rscr4],
           inc="V")
        m1b = m1.rearrange("p (a b) -> p a b", b=1).broadcast_to((128, 4, 8))
        vx(lambda E: E.tensor_tensor(eq.rearrange("p (a b) -> p a b", b=8), et, m1b,
                                     ALU.is_equal),
           reads=[Rb[5], Rscr4], writes=[Rscrf], inc="V")
        vx(lambda E: E.scalar_tensor_tensor(dd, eq, -1e9, etf, ALU.mult, ALU.add),
           reads=[Rscrf, Rb[5]], writes=[Rscrf], inc="V")
        vx(lambda E: E.reduce_max(m2, dd.rearrange("p (a b) -> p a b", b=8),
                                  axis=AX.X),
           reads=[Rscrf], writes=[Rscr4], inc="V")
        m2b = m2.rearrange("p (a b) -> p a b", b=1).broadcast_to((128, 4, 8))
        vx(lambda E: E.tensor_tensor(eq.rearrange("p (a b) -> p a b", b=8), et, m2b,
                                     ALU.is_ge),
           reads=[Rb[5], Rscr4], writes=[Rscrf], inc="V")
        vx(lambda E: E.tensor_tensor(dd.rearrange("p (a b) -> p a b", b=8),
                                     eq.rearrange("p (a b) -> p a b", b=8), et,
                                     ALU.mult),
           reads=[Rscrf, Rb[5]], writes=[Rscrf], inc="V")
        rztb = rzt.rearrange("p (a b) -> p a b", b=1).broadcast_to((128, 4, 8))
        vx(lambda E: E.tensor_tensor(ctm.rearrange("p (a b) -> p a b", b=8),
                                     dd.rearrange("p (a b) -> p a b", b=8), rztb,
                                     ALU.mult),
           reads=[Rscrf, Rscr4], writes=[Rctm], inc="V")
        for u in range(4):
            tin = ctm[:, u * 8: u * 8 + 8]
            out = pbb[4][0:8, u * 128: u * 128 + 128]
            tx(lambda E, i=tin, o=out, u=u: E.matmul(
                o, i, id_b[:], start=(u == 0), stop=(u == 3), is_transpose=True),
               reads=[Rctm, Rconst], writes=[Rb[4]],
               inc=("T" if u == 3 else None))
        vx(lambda E: E.tensor_copy(cfm[0:8, :], pbb[4][0:8, 0:512]),
           reads=[Rb[4]], writes=[Rcfm], inc="V")
        yield
        for er in range(16):
            e = er // 2
            mslot = e % 2
            msb = mwsb[:, mslot * 512: mslot * 512 + 512]
            if er % 2 == 0:
                mwb = 6 + mslot
                lhs = basis[0:8, e * 128: e * 128 + 128]
                tx(lambda E, l=lhs, b=mwb: E.matmul(pb[b][:], l, cfm[0:8, :],
                                                    start=True, stop=True),
                   reads=[Rcfm, Rconst], writes=[Rb[mwb]], inc="T")
                sx(lambda E, b=mwb, o=msb: E.copy(o, pb[b][:]),
                   reads=[Rb[mwb]], writes=[Rmwsb[mslot]], inc="S")
            bank = 4 + (er % 2)
            for k in range(NCH):
                lhs = gw1a[:, k * 2048 + er * 128: k * 2048 + er * 128 + 128]
                rhs = ynbf[:, k * 1024 + scb: k * 1024 + scb + 512]
                tx(lambda E, l=lhs, r=rhs, b=bank, k=k: E.matmul(
                    pb[b][:], l, r, start=(k == 0), stop=(k == 7)),
                   reads=[Rgw1, Rynbf], writes=[Rb[bank]],
                   inc=("T" if k == 7 else None))
            tsl = tbuf[:, er * 512: er * 512 + 512]
            vx(lambda E, b=bank, o=tsl, m=msb: E.tensor_mul(o, pb[b][:], m),
               reads=[Rb[bank], Rmwsb[mslot]], writes=[Rtb], inc="V")
            if er % 2 == 1:
                yield
        P.emit("gpsimd", lambda E: E.dma_start(
            out=gw1a.rearrange("p (k c) -> p k c", c=1024), in_=gwb_src),
            reads=[Rsgw], writes=[Rgw1], inc="dgw1", k=16)
        def issue_w2(s16):
            reg = s16 % 2
            dst = w2a[:, reg * 4096: reg * 4096 + 4096].rearrange(
                "p (k c) -> p k c", c=256)
            if first:
                dmaw("dw2a" if reg == 0 else "dw2b",
                     lambda E, d=dst, s16=s16: E.dma_start(
                         out=d, in_=w2_src[:, s16]),
                     writes=[Rw2[reg]])
                wb_store(dst, w2b_src[:, s16], Rw2[reg], Rsw2,
                         "dsw2" if reg == 0 else "dsw2b")
            else:
                P.emit("gpsimd",
                       lambda E, d=dst, s16=s16: E.dma_start(
                           out=d, in_=w2b_src[:, s16]),
                       reads=[Rsw2], writes=[Rw2[reg]],
                       inc=("dw2a" if reg == 0 else "dw2b"), k=16)

        issue_w2(0)
        issue_w2(1)
        for g in range(4):
            for s16 in range(4 * g, 4 * g + 4):
                reg = s16 % 2
                for f2 in range(2):
                    f = s16 * 2 + f2
                    bank = 4 + (f % 2)
                    for k in range(16):
                        lhs = w2a[:, reg * 4096 + k * 256 + f2 * 128:
                                  reg * 4096 + k * 256 + f2 * 128 + 128]
                        rhs = tbuf[:, k * 512: k * 512 + 512]
                        tx(lambda E, l=lhs, r=rhs, b=bank, k=k: E.matmul(
                            pb[b][:], l, r, start=(k == 0), stop=(k == 15)),
                           reads=[Rw2[reg], Rtb], writes=[Rb[bank]],
                           inc=("T" if k == 15 else None))
                    hs = f % 8
                    hsl = ah[:, HQ0 + hs * 512: HQ0 + hs * 512 + 512]
                    sx(lambda E, b=bank, o=hsl: E.activation(o, pb[b][:], AF.Gelu),
                       reads=[Rb[bank]], writes=[Rhq[hs]], inc="S")
                    if f2 == 1 and s16 + 2 < 16:
                        issue_w2(s16 + 2)
                    yield
            if first and g == 3:
                Rsw2.writers = {"dsw2": P.cnt["dsw2"], "dsw2b": P.cnt["dsw2b"]}
            for mh in range(2):
                for kfl in range(8):
                    kf = 8 * g + kfl
                    slot = (kfl // 2) % 2
                    if kfl % 2 == 0:
                        dst = l2a[:, slot * 2048: slot * 2048 + 2048].rearrange(
                            "p (k c) -> p k c", c=1024)
                        if first and mh == 0:
                            dmaw("dl2a" if slot == 0 else "dl2b",
                                 lambda E, d=dst, kf=kf: E.dma_start(
                                     out=d, in_=l2_src[:, kf:kf + 2, :]),
                                 writes=[Rl2[slot]])
                            wb_store(dst, l2b_src[:, kf:kf + 2, :], Rl2[slot],
                                     Rsl2, "dsl2s" if slot == 0 else "dsl2t")
                        else:
                            rd = [] if (first and mh == 1 and g == 0) else [Rsl2]
                            if first:
                                rd = [Rl2s_part[g]]
                            P.emit("gpsimd",
                                   lambda E, d=dst, kf=kf: E.dma_start(
                                       out=d, in_=l2b_src[:, kf:kf + 2, :]),
                                   reads=rd, writes=[Rl2[slot]],
                                   inc=("dl2a" if slot == 0 else "dl2b"), k=16)
                    off = (kfl % 2) * 1024
                    hs = kf % 8
                    rhs = ah[:, HQ0 + hs * 512: HQ0 + hs * 512 + 512]
                    for mm in range(4):
                        m = mh * 4 + mm
                        bank = 4 + mm
                        lhs = l2a[:, slot * 2048 + off + m * 128:
                                  slot * 2048 + off + m * 128 + 128]
                        tx(lambda E, l=lhs, r=rhs, b=bank, kfl=kfl: E.matmul(
                            pb[b][:], l, r, start=(kfl == 0), stop=(kfl == 7)),
                           reads=[Rl2[slot], Rhq[hs]], writes=[Rb[bank]],
                           inc=("T" if (kfl % 2 == 1 and mm == 3) else None))
                if first and mh == 0:
                    Rl2s_part[g].writers = {"dsl2s": P.cnt["dsl2s"],
                                            "dsl2t": P.cnt["dsl2t"]}
                for mm in range(4):
                    m = mh * 4 + mm
                    bank = 4 + mm
                    acc = l2acc[:, m * 512: m * 512 + 512]
                    if g == 0:
                        vx(lambda E, b=bank, a=acc: E.tensor_copy(a, pb[b][:]),
                           reads=[Rb[bank]], writes=[Rl2acc], inc="V")
                    elif g < 3:
                        vx(lambda E, b=bank, a=acc: E.tensor_add(a, pb[b][:], a),
                           reads=[Rb[bank], Rl2acc], writes=[Rl2acc], inc="V")
                    else:
                        q1 = scrf[:, 0:512]
                        ysl = ynbf[:, m * 1024 + scb: m * 1024 + scb + 512]
                        vx(lambda E, b=bank, a=acc, o=q1: E.tensor_add(o, pb[b][:], a),
                           reads=[Rb[bank], Rl2acc], writes=[Rscrf], inc="V")
                        vx(lambda E, y=ysl, m=m, i=q1: E.scalar_tensor_tensor(
                            y, i, lb_s[:, m:m + 1], y, ALU.add, ALU.add),
                           reads=[Rscrf, Rlb, Rynbf], writes=[Rynbf], inc="V")
                yield
        if first:
            Rsl2.writers = {"dsl2s": P.cnt["dsl2s"], "dsl2t": P.cnt["dsl2t"]}
        for _ in ln_stage(scb, "ln2", t, (4, 5, 6, 7),
                          lambda fc: (ah[:, HQ0 + (fc % 2) * 512:
                                         HQ0 + (fc % 2) * 512 + 512],
                                      Rhq[fc % 2])):
            yield
        yield

    prev_ffn = None
    for t in range(NT):
        tb0 = t * 1024
        for sc in range(2):
            scb = sc * 512
            dmaw("dpf0", lambda E, s=xfm_g[:, :, tb0 + scb: tb0 + scb + 512],
                 d=xpef.rearrange("p (f s) -> p f s", s=512): E.dma_start(
                     out=d, in_=s),
                 writes=[Rxpef])
            dmaw("dpf1", lambda E, s=pfm_g[:, :, tb0 + scb: tb0 + scb + 512],
                 d=xpef.rearrange("p (f s) -> p f s", s=512): E.dma_start(
                     out=d, in_=s, accum_op=ALU.add),
                 writes=[Rxpef])
            g0 = (tb0 + scb) // 128
            dmaw("dsl0", lambda E, s=xtm_g[:, g0:g0 + 4, :],
                 d=xpet.rearrange("p (g d) -> p g d", d=1024): E.dma_start(
                     out=d, in_=s),
                 writes=[Rxpet])
            dmaw("dsl1", lambda E, s=ptm_g[:, g0:g0 + 4, :],
                 d=xpet.rearrange("p (g d) -> p g d", d=1024): E.dma_start(
                     out=d, in_=s, accum_op=ALU.add),
                 writes=[Rxpet])
            def attn_e(t=t, sc=sc):
                for _ in attn_gen(t, sc):
                    yield
                for _ in e_phase(t, sc):
                    yield
            if prev_ffn is not None:
                interleave([(prev_ffn, 1)])
            interleave([(attn_e(), 1)])
            prev_ffn = ffn_gen(t, sc)
    interleave([(prev_ffn, 1)])

    P.raw_wait("sync", {"dout0": P.cnt["dout0"], "dout1": P.cnt["dout1"]})

    semh = {}
    for name in ("T", "V", "S", "G") + DMA_SEMS:
        semh[name] = stack.enter_context(nc.semaphore(name))

    ctx_lp = nc.allow_low_precision(
        reason="bf16 operands for matmuls are intended; fp32 PSUM accumulation")
    ctx_lp.__enter__()
    with nc.Block() as block:
        def run_queue(qname):
            def runner(E):
                for item in P.q[qname]:
                    if item[0] == "w":
                        E.wait_ge(semh[item[1]], item[2])
                    else:
                        _, fn, inc, k = item
                        ins = fn(E)
                        if inc is not None:
                            ins.then_inc(semh[inc], k)
            return runner

        block.tensor(run_queue("tensor"))
        block.vector(run_queue("vector"))
        block.scalar(run_queue("scalar"))
        block.sync(run_queue("sync"))
        block.gpsimd(run_queue("gpsimd"))

    ctx_lp.__exit__(None, None, None)
    stack.close()
    return nc


_NC_CACHE = None


def kernel(L0_attn_x, L0_attn_pos_emb, L0_attn_global_memory, L0_attn_conv_w,
           L0_attn_conv_b, L0_attn_gate_w, L0_attn_gate_b, L0_attn_out_w,
           L0_attn_out_b, L0_ffn_router_w, L0_ffn_router_b, L0_ffn_w1,
           L0_ffn_w2, L0_ffn_lin2_w, L0_ffn_lin2_b):
    global _NC_CACHE
    if _NC_CACHE is None:
        _NC_CACHE = build_nc()
    nc = _NC_CACHE

    f32 = np.float32
    ac = np.ascontiguousarray
    pos = ac(L0_attn_pos_emb[:S], dtype=f32)
    pos_f = ac(pos.T)
    convw = np.concatenate(
        [L0_attn_conv_w.transpose(2, 1, 0).reshape(4096, D),
         L0_attn_conv_b.reshape(1, D)], axis=0)
    convw = ac(convw, dtype=f32)
    gate_w = ac(L0_attn_gate_w, dtype=f32)
    out_w = ac(L0_attn_out_w, dtype=f32)
    w1fh = ac(L0_ffn_w1.transpose(1, 0, 2).reshape(D, 2048), dtype=f32)
    w2fh = ac(L0_ffn_w2.reshape(2048, FF), dtype=f32)
    lin2 = ac(L0_ffn_lin2_w, dtype=f32)
    gb = ac(L0_attn_gate_b.reshape(8, 128).T, dtype=f32)
    ob = ac(L0_attn_out_b.reshape(8, 128).T, dtype=f32)
    lb = ac(L0_ffn_lin2_b.reshape(8, 128).T, dtype=f32)
    rw = ac(L0_ffn_router_w.reshape(8, 128, 8).transpose(1, 0, 2).reshape(128, 64),
            dtype=f32)
    rb = ac(L0_ffn_router_b.reshape(8, 1), dtype=f32)
    gm = ac(L0_attn_global_memory[0], dtype=f32)

    in_maps = []
    for b in range(8):
        xb = ac(L0_attn_x[b], dtype=f32)
        in_maps.append({
            "x_tm": xb, "x_fm": ac(xb.T),
            "pos_tm": pos, "pos_fm": pos_f,
            "gmem": gm, "convw": convw,
            "gate_w": gate_w, "out_w": out_w,
            "w1f": w1fh, "w2f": w2fh, "lin2_w": lin2,
            "gate_b": gb, "out_b": ob, "lin2_b": lb,
            "router_w": rw, "router_b": rb,
        })
    res = run_bass_kernel_spmd(nc, in_maps, core_ids=list(range(8)))
    out = np.empty((8, S, D), np.float32)
    for b in range(8):
        out[b] = res.results[b]["out_fm"].T
    return out


# version 11
# speedup vs baseline: 1.1534x; 1.1534x over previous
import numpy as np
import concourse.bass as bass
import concourse.mybir as mybir
from concourse.bass_utils import run_bass_kernel_spmd

F32 = mybir.dt.float32
BF16 = mybir.dt.bfloat16
AF = mybir.ActivationFunctionType
ALU = mybir.AluOpType
AX = mybir.AxisListType

S, D, H, HD, WIN = 4096, 1024, 16, 64, 512
TILE, NT, SC = 1024, 4, 512
NCH = 8
E8, RR, FF = 8, 256, 4096
SCALE = 0.125
CEXP = -8.0

ENGINES = ("tensor", "vector", "scalar", "sync", "gpsimd")


class Reg:
    __slots__ = ("writers", "readers")

    def __init__(self):
        self.writers = {}
        self.readers = {}

    def wrote(self, gate):
        s, v = gate
        self.writers = {s: v}
        self.readers = {}

    def read(self, gate):
        s, v = gate
        if self.readers.get(s, 0) < v:
            self.readers[s] = v

def alias_edge(dsts, srcs):
    merged = {}
    for o in srcs:
        for d in (o.writers, o.readers):
            for s, v in d.items():
                if merged.get(s, 0) < v:
                    merged[s] = v
    for o in srcs:
        o.writers = {}
        o.readers = {}
    for r in dsts:
        r.writers = {}
        r.readers = dict(merged)


class Prog:
    def __init__(self):
        self.q = {e: [] for e in ENGINES}
        self.cnt = {}
        self.guar = {e: {} for e in ENGINES}
        self.pend = {e: [] for e in ENGINES}

    def sem(self, name):
        self.cnt[name] = 0
        return name

    def _wait(self, eng, sname, val):
        if val <= 0:
            return
        if self.guar[eng].get(sname, 0) >= val:
            return
        self.guar[eng][sname] = val
        self.q[eng].append(("w", sname, val))

    def emit(self, eng, fn, reads=(), writes=(), inc=None, k=1):
        need = {}
        for r in reads:
            for s, v in r.writers.items():
                if need.get(s, 0) < v:
                    need[s] = v
        for r in writes:
            for d in (r.writers, r.readers):
                for s, v in d.items():
                    if need.get(s, 0) < v:
                        need[s] = v
        if inc is not None and inc[0] == "d":
            need.pop(inc, None)
        for s, v in need.items():
            self._wait(eng, s, v)
        if inc is None:
            self.q[eng].append(("op", fn, None, 0))
            self.pend[eng].append((tuple(reads), tuple(writes)))
            return None
        self.cnt[inc] += k
        gate = (inc, self.cnt[inc])
        if eng == "tensor" and inc == "T":
            self.guar[eng][inc] = self.cnt[inc]
        self.q[eng].append(("op", fn, inc, k))
        for rr, ww in self.pend[eng]:
            for r in rr:
                r.read(gate)
            for w in ww:
                w.wrote(gate)
        self.pend[eng].clear()
        for r in reads:
            r.read(gate)
        for r in writes:
            r.wrote(gate)
        return gate

    def raw_wait(self, eng, gatedict):
        for s, v in gatedict.items():
            self._wait(eng, s, v)


def build_nc():
    nc = bass.Bass("TRN2", target_bir_lowering=False, debug=False)
    P = Prog()
    for s in ("T", "V", "S", "G"):
        P.sem(s)
    DMA_SEMS = ("dxy", "dsl0", "dsl1", "dpf0", "dpf1", "dgw1", "doutw",
                "dw2a", "dw2b", "dl2a", "dl2b", "dmisc", "dbias",
                "dout0", "dout1", "dsb", "dsw2", "dsw2b", "dsl2s", "dsl2t")
    for s in DMA_SEMS:
        P.sem(s)

    dt_in = lambda n, sh, dt=F32: nc.dram_tensor(n, sh, dt, kind="ExternalInput").ap()
    x_tm = dt_in("x_tm", [S, D])
    x_fm = dt_in("x_fm", [D, S])
    pos_tm = dt_in("pos_tm", [S, D])
    pos_fm = dt_in("pos_fm", [D, S])
    gmem = dt_in("gmem", [64, D])
    convw = dt_in("convw", [4097, D])
    gate_w = dt_in("gate_w", [2048, D])
    out_w = dt_in("out_w", [D, D])
    w1f = dt_in("w1f", [D, 2048])
    w2f = dt_in("w2f", [2048, FF])
    lin2_w = dt_in("lin2_w", [FF, D])
    gate_b = dt_in("gate_b", [128, 8])
    out_b = dt_in("out_b", [128, 8])
    lin2_b = dt_in("lin2_b", [128, 8])
    router_w = dt_in("router_w", [128, 64])
    router_b = dt_in("router_b", [8, 1])
    out_fm = nc.dram_tensor("out_fm", [D, S], F32, kind="ExternalOutput").ap()
    dt_sc = lambda n, sh: nc.dram_tensor(n, sh, BF16, kind="Internal").ap()
    gw_bf = dt_sc("gw_bf", [2048, D])
    ow_bf = dt_sc("ow_bf", [D, D])
    w1_bf = dt_sc("w1_bf", [D, 2048])
    w2_bf = dt_sc("w2_bf", [2048, FF])
    l2_bf = dt_sc("l2_bf", [FF, D])

    import contextlib
    stack = contextlib.ExitStack()
    sb = lambda n, sh, dt: stack.enter_context(nc.sbuf_tensor(n, sh, dt)).ap()
    ps = lambda n, sh, dt: stack.enter_context(nc.psum_tensor(n, sh, dt)).ap()

    xy = sb("xy", [128, 8192], BF16)
    ynbf = sb("ynbf", [128, 8192], BF16)
    xpef = sb("xpef", [128, 4096], BF16)
    xpet = sb("xpet", [128, 4096], BF16)
    ah = sb("ah", [128, 16384], BF16)
    tbuf = sb("tbuf", [128, 8192], BF16)
    gw1a = sb("gw1a", [128, 16384], BF16)
    outwa = sb("outwa", [128, 8192], BF16)
    w2a = sb("w2a", [128, 16384], BF16)
    gt_tm = sb("gt_tm", [128, 1024], BF16)

    gt_fm = sb("gt_fm", [128, 1024], BF16)
    ebf = sb("ebf", [128, 512], F32)
    cfm = sb("cfm", [128, 512], BF16)
    ctm = sb("ctm", [128, 32], F32)
    csb = sb("csb", [128, 512], BF16)
    mwsb = sb("mwsb", [128, 1024], BF16)
    ztm = sb("ztm", [128, 32], F32)
    rztm = sb("rztm", [128, 32], BF16)
    scr4 = sb("scr4", [128, 16], F32)
    scrf = sb("scrf", [128, 1024], F32)
    gt_tf = scrf
    statv = sb("statv", [1, 2048], F32)
    ones_b = sb("ones_b", [128, 128], BF16)
    ones_f = sb("ones_f", [128, 128], F32)
    id_f = sb("id_f", [128, 128], F32)
    id_b = sb("id_b", [128, 128], BF16)
    basis = sb("basis", [8, 1024], BF16)
    neg8 = sb("neg8", [128, 1], F32)
    epsc = sb("epsc", [128, 1], F32)
    gb_s = sb("gb_s", [128, 8], F32)
    ob_s = sb("ob_s", [128, 8], F32)
    lb_s = sb("lb_s", [128, 8], F32)
    rb_s = sb("rb_s", [8, 1], F32)
    rw_s = sb("rw_s", [128, 64], BF16)
    cbrow = sb("cbrow", [1, 1024], BF16)
    iot = sb("iot", [128, 1], F32)

    pb = [ps(f"pb{i}", [128, 512], F32) for i in range(8)]
    pbb = [p.bitcast(BF16) for p in pb]

    LOC, GLO, MIX, PL0, PG0, RZ0 = 0, 4096, 8192, 12288, 14336, 15360

    R = lambda: Reg()
    Rxy, Rynbf, Rxpef, Rxpet = R(), R(), R(), R()
    Rloc, Rglo, Rmix = R(), R(), R()
    Rpl = [R() for _ in range(4)]
    Rpg = [R() for _ in range(2)]
    Rrz = [R() for _ in range(2)]
    Rah = R()
    Rtb = R()
    Rslab = [R() for _ in range(6)]
    Rgw1, Routw = R(), R()
    Rw2 = [R() for _ in range(2)]
    Rconv = [R() for _ in range(4)]
    Rl2 = [R() for _ in range(4)]
    Rout = [R() for _ in range(2)]
    Rgt_tm, Rgt_fm, Rgt_tf = R(), R(), R()
    Rgb, Rob, Rlb, Rrb, Rrw, Rcbrow = R(), R(), R(), R(), R(), R()
    Rsgw, Rsow, Rsw1, Rsw2, Rsl2 = R(), R(), R(), R(), R()
    Rebf, Rcfm, Rctm, Rscr4, Rscrf, Rstat = R(), R(), R(), R(), R(), R()
    Rzt = [R(), R()]
    Rrzt = [R(), R()]
    Rcsb = R()
    Rmwsb = [R(), R()]
    Rb = [R() for _ in range(8)]
    Rconst = R()
    Rbias = R()

    outst = xpef.bitcast(F32)

    def gp(fn, **kw):
        return P.emit("gpsimd", fn, **kw)

    def vx(fn, **kw):
        return P.emit("vector", fn, **kw)

    def sx(fn, **kw):
        return P.emit("scalar", fn, **kw)

    def tx(fn, **kw):
        return P.emit("tensor", fn, **kw)

    def dmaw(sem, fn, **kw):
        return P.emit("gpsimd", fn, inc=sem, k=16, **kw)

    def dmax(sem, fn, **kw):
        return P.emit("sync", fn, inc=sem, k=16, **kw)

    gp(lambda E: E.memset(ones_b[:], 1.0), writes=[Rconst], inc="G")
    gp(lambda E: E.memset(ones_f[:], 1.0), writes=[Rconst], inc="G")
    gp(lambda E: E.memset(neg8[:], CEXP), writes=[Rconst], inc="G")
    gp(lambda E: E.memset(epsc[:], 1e-5), writes=[Rconst], inc="G")
    gp(lambda E: E.iota(id_f[:], pattern=[[1, 128]], channel_multiplier=0,
                        allow_small_or_imprecise_dtypes=True),
       writes=[Rconst], inc="G")
    gp(lambda E: E.iota(iot[:], pattern=[[0, 1]], channel_multiplier=1,
                        allow_small_or_imprecise_dtypes=True),
       writes=[Rconst], inc="G")
    vx(lambda E: E.tensor_scalar(id_f[:], id_f[:], iot[:], None, ALU.is_equal),
       reads=[Rconst], writes=[Rconst], inc="V")
    vx(lambda E: E.tensor_copy(id_b[:], id_f[:]), reads=[Rconst], writes=[Rconst], inc="V")
    gp(lambda E: E.iota(basis[:], pattern=[[1, 8], [0, 128]], channel_multiplier=0,
                        allow_small_or_imprecise_dtypes=True),
       writes=[Rconst], inc="G")
    vx(lambda E: E.tensor_scalar(basis[:], basis[:], iot[0:8, :], None, ALU.is_equal),
       reads=[Rconst], writes=[Rconst], inc="V")
    dmax("dbias", lambda E: E.dma_start(out=gb_s[:], in_=gate_b), writes=[Rgb])
    dmax("dbias", lambda E: E.dma_start(out=ob_s[:], in_=out_b), writes=[Rob])
    dmax("dbias", lambda E: E.dma_start(out=lb_s[:], in_=lin2_b), writes=[Rlb])
    dmax("dbias", lambda E: E.dma_start(out=rb_s[:], in_=router_b), writes=[Rrb])
    dmaw("dmisc", lambda E: E.dma_start(out=rw_s[:], in_=router_w), writes=[Rrw])
    dmaw("dmisc", lambda E: E.dma_start(out=gt_tm[64:128, :], in_=gmem), writes=[Rgt_tm])
    dmax("dbias", lambda E: E.dma_start(out=gt_tf[64:128, :], in_=gmem), writes=[Rgt_tf])
    dmaw("dmisc", lambda E: E.dma_start(out=cbrow[:], in_=convw[4096:4097, :]), writes=[Rcbrow])
    conv_src = convw[0:4096, :].rearrange("(j p) c -> p j c", p=128)

    def conv_group_load(g):
        slot = g % 2
        dst = w2a[:, slot * 4096:(slot + 1) * 4096].rearrange("p (j c) -> p j c", c=1024)
        dmaw("dw2a" if slot == 0 else "dw2b",
             lambda E, g=g, d=dst: E.dma_start(out=d, in_=conv_src[:, 4 * g:4 * g + 4, :]),
             writes=[Rconv[slot]])

    for r in (Rgb, Rob, Rlb, Rrb, Rgt_tf):
        r.writers = {"dbias": P.cnt["dbias"]}
    for r in (Rrw, Rgt_tm, Rcbrow):
        r.writers = {"dmisc": P.cnt["dmisc"]}

    conv_group_load(0)
    conv_group_load(1)
    gw_src = gate_w.rearrange("(k p) c -> p k c", p=128)
    w1_src = w1f.rearrange("(k p) c -> p k c", p=128)
    l2_src = lin2_w.rearrange("(k p) c -> p k c", p=128)
    w2_src = w2f.rearrange("(k p) (e c) -> p e k c", p=128, c=512)
    gwb_src = gw_bf.rearrange("(k p) c -> p k c", p=128)
    owb_src = ow_bf.rearrange("(k p) c -> p k c", p=128)
    w1b_src = w1_bf.rearrange("(k p) c -> p k c", p=128)
    l2b_src = l2_bf.rearrange("(k p) c -> p k c", p=128)
    w2b_src = w2_bf.rearrange("(k p) (e c) -> p e k c", p=128, c=512)

    def wb_store(sbuf_view, dst_view, slab_reg, scr_reg, sem="dsb"):
        P.emit("sync", lambda E, s=sbuf_view, d=dst_view: E.dma_start(out=d, in_=s),
               reads=[slab_reg], writes=[scr_reg], inc=sem, k=16)

    dmaw("dgw1", lambda E: E.dma_start(out=gw1a.rearrange("p (k c) -> p k c", c=1024),
                               in_=gw_src),
         writes=[Rgw1])
    wb_store(gw1a.rearrange("p (k c) -> p k c", c=1024), gwb_src, Rgw1, Rsgw)
    ow_src = out_w.rearrange("(k p) c -> p k c", p=128)
    dmaw("doutw", lambda E: E.dma_start(out=outwa.rearrange("p (k c) -> p k c", c=1024),
                                in_=ow_src),
         writes=[Routw])
    wb_store(outwa.rearrange("p (k c) -> p k c", c=1024), owb_src, Routw, Rsow)
    for r in (Rsgw, Rsow):
        r.writers = {"dsb": P.cnt["dsb"]}
    Rgw1.readers["dsb"] = P.cnt["dsb"]
    Routw.readers["dsb"] = P.cnt["dsb"]


    def ln_stage(sc_base, out_kind, t, bank_ids, scratch_reg):
        bsy, bsq, bb0, bb1 = bank_ids
        srct, srcr = (xy, Rxy) if out_kind == "ln1" else (ynbf, Rynbf)
        for fc in range(NCH):
            xsl = srct[:, fc * 1024 + sc_base: fc * 1024 + sc_base + 512]
            pl = ah[:, PL0 + (fc % 2) * 512: PL0 + (fc % 2) * 512 + 512]
            srg = scratch_reg(fc)
            sx(lambda E, i=xsl, o=pl: E.activation(o, i, AF.Square),
               reads=[srcr], writes=[srg], inc="S")
            tx(lambda E, r=xsl, fc=fc: E.matmul(pb[bsy][0:1, :], ones_b[:, 0:1], r,
                                                start=(fc == 0), stop=(fc == 7)),
               reads=[srcr, Rconst], writes=[Rb[bsy]], inc="T")
            tx(lambda E, r=pl, fc=fc: E.matmul(pb[bsq][0:1, :], ones_b[:, 0:1], r,
                                               start=(fc == 0), stop=(fc == 7)),
               reads=[srg, Rconst], writes=[Rb[bsq]], inc="T")
        mean = statv[0:1, 0:512]
        inv = statv[0:1, 512:1024]
        minv = statv[0:1, 1024:1536]
        var = statv[0:1, 1536:2048]
        vx(lambda E: E.tensor_scalar(mean, pb[bsy][0:1, :], 1.0 / 1024, None, ALU.mult),
           reads=[Rb[bsy]], writes=[Rstat], inc="V")
        vx(lambda E: E.tensor_mul(var, mean, mean), reads=[Rstat], writes=[Rstat], inc="V")
        vx(lambda E: E.scalar_tensor_tensor(var, pb[bsq][0:1, :], 1.0 / 1024, var,
                                            ALU.mult, ALU.subtract),
           reads=[Rb[bsq], Rstat], writes=[Rstat], inc="V")
        sx(lambda E: E.activation(var, var, AF.Ln, bias=epsc[0:1, :]),
           reads=[Rstat, Rconst], writes=[Rstat], inc="S")
        sx(lambda E: E.activation(inv, var, AF.Exp, scale=-0.5),
           reads=[Rstat], writes=[Rstat], inc="S")
        vx(lambda E: E.tensor_mul(minv, mean, inv), reads=[Rstat], writes=[Rstat], inc="V")
        invb = statv.bitcast(BF16)[0:1, 3072:3584]
        minvb = statv.bitcast(BF16)[0:1, 3584:4096]
        vx(lambda E: E.tensor_copy(invb, inv), reads=[Rstat], writes=[Rstat], inc="V")
        vx(lambda E: E.tensor_copy(minvb, minv), reads=[Rstat], writes=[Rstat], inc="V")
        tx(lambda E: E.matmul(pb[bb0][:], ones_b[0:1, :], invb, start=True, stop=True),
           reads=[Rstat, Rconst], writes=[Rb[bb0]], inc="T")
        tx(lambda E: E.matmul(pb[bb1][:], ones_b[0:1, :], minvb, start=True, stop=True),
           reads=[Rstat, Rconst], writes=[Rb[bb1]], inc="T")
        for fc in range(NCH):
            xsl = srct[:, fc * 1024 + sc_base: fc * 1024 + sc_base + 512]
            q1 = scrf[:, (fc % 2) * 512:(fc % 2) * 512 + 512]
            vx(lambda E, i=xsl, o=q1: E.tensor_mul(o, i, pb[bb0][:]),
               reads=[srcr, Rb[bb0]], writes=[Rscrf], inc="V")
            if out_kind == "ln1":
                osl = ynbf[:, fc * 1024 + sc_base: fc * 1024 + sc_base + 512]
                vx(lambda E, i=q1, o=osl: E.tensor_sub(o, i, pb[bb1][:]),
                   reads=[Rscrf, Rb[bb1]], writes=[Rynbf], inc="V")
            else:
                slot = fc % 2
                osl = outst[:, slot * 512: slot * 512 + 512]
                vx(lambda E, i=q1, o=osl: E.tensor_sub(o, i, pb[bb1][:]),
                   reads=[Rscrf, Rb[bb1]], writes=[Rout[slot]], inc="V")
                dst = out_fm[fc * 128:(fc + 1) * 128,
                             t * 1024 + sc_base: t * 1024 + sc_base + 512]
                P.emit("sync", lambda E, s=osl, d=dst: E.dma_start(out=d, in_=s),
                       reads=[Rout[slot]], inc="dout%d" % slot, k=16)

    for t in range(NT):
        tb0 = t * 1024
        if t > 0:
            alias_edge([Rxpet], Rl2)
            alias_edge([Rxpef], Rout)
            alias_edge(Rpl + Rpg + Rrz + [Rloc, Rglo, Rmix], [Rah])
        for fc in range(NCH):
            src = x_fm[fc * 128:(fc + 1) * 128, tb0:tb0 + 1024]
            dst = xy[:, fc * 1024: fc * 1024 + 1024]
            dmaw("dxy", lambda E, s=src, d=dst: E.dma_start(out=d, in_=s), writes=[Rxy])

        xfm_g = x_fm.rearrange("(f p) s -> p f s", p=128)
        pfm_g = pos_fm.rearrange("(f p) s -> p f s", p=128)
        xtm_g = x_tm.rearrange("(g p) d -> p g d", p=128)
        ptm_g = pos_tm.rearrange("(g p) d -> p g d", p=128)
        for sc in range(2):
            scb = sc * 512
            dmaw("dpf0", lambda E, s=xfm_g[:, :, tb0 + scb: tb0 + scb + 512],
                 d=xpef.rearrange("p (f s) -> p f s", s=512): E.dma_start(out=d, in_=s),
                 writes=[Rxpef])
            dmaw("dpf1", lambda E, s=pfm_g[:, :, tb0 + scb: tb0 + scb + 512],
                 d=xpef.rearrange("p (f s) -> p f s", s=512): E.dma_start(
                     out=d, in_=s, accum_op=ALU.add),
                 writes=[Rxpef])
            g0 = (tb0 + scb) // 128
            dmaw("dsl0", lambda E, s=xtm_g[:, g0:g0 + 4, :],
                 d=xpet.rearrange("p (g d) -> p g d", d=1024): E.dma_start(out=d, in_=s),
                 writes=[Rxpet])
            dmaw("dsl1", lambda E, s=ptm_g[:, g0:g0 + 4, :],
                 d=xpet.rearrange("p (g d) -> p g d", d=1024): E.dma_start(
                     out=d, in_=s, accum_op=ALU.add),
                 writes=[Rxpet])

            scnt = 0
            pcnt = 0
            for q in range(4):
                qs = q % 2
                for hh in range(4):
                    h = q * 4 + hh
                    fc, pbase = h // 2, 64 * (h % 2)
                    zsl = ztm[:, qs * 16 + hh * 4: qs * 16 + hh * 4 + 4]
                    for kc in range(4):
                        b = scnt % 3
                        scnt += 1
                        lhs = xpef[pbase:pbase + 64,
                                   fc * 512 + kc * 128: fc * 512 + kc * 128 + 128]
                        rhs = xpef[pbase:pbase + 64, fc * 512: fc * 512 + 512]
                        tx(lambda E, l=lhs, r=rhs, b=b: E.matmul(
                            pb[b][:], l, r, start=True, stop=True),
                           reads=[Rxpef], writes=[Rb[b]], inc="T")
                        slot = pcnt % 4
                        pcnt += 1
                        pls = ah[:, PL0 + slot * 512: PL0 + slot * 512 + 512]
                        zc = zsl[:, kc:kc + 1]
                        sx(lambda E, b=b, o=pls, z=zc: E.activation(
                            o, pb[b][:], AF.Exp, bias=neg8[:], scale=SCALE,
                            accum_out=z),
                           reads=[Rb[b], Rconst], writes=[Rpl[slot], Rzt[qs]],
                           inc="S")
                    av_bank = 3 + hh
                    for kc in range(4):
                        slot = (pcnt - 4 + kc) % 4
                        pls = ah[:, PL0 + slot * 512: PL0 + slot * 512 + 512]
                        lhs = xpet[:, kc * 1024 + 64 * h: kc * 1024 + 64 * h + 64]
                        out = pb[av_bank][pbase:pbase + 64, :]
                        tx(lambda E, l=lhs, r=pls, o=out, kc=kc, ab=av_bank: E.matmul(
                            o, l, r, start=(kc == 0), stop=(kc == 3)),
                           reads=[Rpl[slot], Rxpet], writes=[Rb[av_bank]],
                           inc=("T" if kc == 3 else None))
                vx(lambda E, qs=qs: E.reciprocal(rztm[:, qs * 16: qs * 16 + 16],
                                                 ztm[:, qs * 16: qs * 16 + 16]),
                   reads=[Rzt[qs]], writes=[Rrzt[qs]], inc="V")
                for hh in range(4):
                    h = q * 4 + hh
                    fc, pbase = h // 2, 64 * (h % 2)
                    for kc in range(4):
                        rc = rztm[:, qs * 16 + hh * 4 + kc: qs * 16 + hh * 4 + kc + 1]
                        out = pbb[7][0:1, kc * 128:kc * 128 + 128]
                        tx(lambda E, i=rc, o=out, kc=kc: E.matmul(
                            o, i, id_b[:], start=(kc == 0), stop=(kc == 3),
                            is_transpose=True),
                           reads=[Rrzt[qs], Rconst], writes=[Rb[7]],
                           inc=("T" if kc == 3 else None))
                    rzslot = h % 2
                    rzv = ah[0:1, RZ0 + rzslot * 512: RZ0 + rzslot * 512 + 512]
                    vx(lambda E, o=rzv: E.tensor_copy(o, pbb[7][0:1, 0:512]),
                       reads=[Rb[7]], writes=[Rrz[rzslot]], inc="V")
                    if hh % 2 == 1:
                        rzA = ah[0:1, RZ0: RZ0 + 512]
                        tx(lambda E, r=rzA: E.matmul(pb[7][0:64, :],
                                                     ones_b[0:1, 0:64], r,
                                                     start=True, stop=True),
                           reads=[Rrz[0], Rconst], writes=[Rb[7]], inc=None)
                        tx(lambda E, r=rzv: E.matmul(pb[7][64:128, :],
                                                     ones_b[0:1, 0:64], r,
                                                     start=True, stop=True),
                           reads=[Rrz[1], Rconst], writes=[Rb[7]], inc="T")
                        vx(lambda E: E.tensor_copy(csb[:], pb[7][:]),
                           reads=[Rb[7]], writes=[Rcsb], inc="V")
                        avA, avB = 3 + hh - 1, 3 + hh
                        lA = ah[0:64, LOC + fc * 512: LOC + fc * 512 + 512]
                        lB = ah[64:128, LOC + fc * 512: LOC + fc * 512 + 512]
                        vx(lambda E, o=lA, ab=avA: E.tensor_mul(
                            o, pb[ab][0:64, :], csb[0:64, :]),
                           reads=[Rb[avA], Rcsb], writes=[Rloc], inc="V")
                        vx(lambda E, o=lB, ab=avB: E.tensor_mul(
                            o, pb[ab][64:128, :], csb[64:128, :]),
                           reads=[Rb[avB], Rcsb], writes=[Rloc], inc="V")

            if t == 0 and sc == 0:
                for j in range(32):
                    kc, ic = j // 8, j % 8
                    g, jj = j // 4, j % 4
                    slot = g % 2
                    if jj == 0 and g >= 2:
                        conv_group_load(g)
                    lhs = xpef[:, ic * 512: ic * 512 + 256].rearrange(
                        "p (t f) -> p t f", f=4)[:, :, kc]
                    for oh in range(2):
                        bank = 3 + oh
                        rhs = w2a[:, slot * 4096 + jj * 1024 + oh * 512:
                                  slot * 4096 + jj * 1024 + oh * 512 + 512]
                        tx(lambda E, l=lhs, r=rhs, b=bank, j=j: E.matmul(
                            pb[b][0:64, :], l, r, start=(j == 0), stop=False),
                           reads=[Rxpef, Rconv[slot]], writes=[Rb[bank]],
                           inc=("T" if (j % 4) == 3 and oh == 1 else None))
                for oh in range(2):
                    bank = 3 + oh
                    rhsb = cbrow[0:1, oh * 512: oh * 512 + 512]
                    tx(lambda E, r=rhsb, b=bank: E.matmul(
                        pb[b][0:64, :], ones_b[0:1, 0:64], r, start=False, stop=True),
                       reads=[Rcbrow, Rconst], writes=[Rb[bank]], inc="T")
                    vx(lambda E, b=bank, oh=oh: E.tensor_copy(
                        gt_tm[0:64, oh * 512: oh * 512 + 512], pb[b][0:64, :]),
                       reads=[Rb[bank]], writes=[Rgt_tm], inc="V")
                    vx(lambda E, b=bank, oh=oh: E.tensor_copy(
                        gt_tf[0:64, oh * 512: oh * 512 + 512], pb[b][0:64, :]),
                       reads=[Rb[bank]], writes=[Rgt_tf], inc="V")
                for fc in range(NCH):
                    bank = 5
                    tin = gt_tf[:, fc * 128: fc * 128 + 128]
                    tx(lambda E, i=tin, b=bank: E.transpose(pb[b][:, 0:128], i, id_f[:]),
                       reads=[Rgt_tf, Rconst], writes=[Rb[bank]], inc="T")
                    vx(lambda E, fc=fc, b=bank: E.tensor_copy(
                        gt_fm[:, fc * 128: fc * 128 + 128], pb[b][:, 0:128]),
                       reads=[Rb[bank]], writes=[Rgt_fm], inc="V")
                alias_edge([Rscrf], [Rgt_tf])

            scnt = 0
            for q in range(4):
                qs = q % 2
                for hh in range(4):
                    h = q * 4 + hh
                    fc, pbase = h // 2, 64 * (h % 2)
                    b = scnt % 2
                    scnt += 1
                    lhs = gt_fm[pbase:pbase + 64, fc * 128: fc * 128 + 128]
                    rhs = xpef[pbase:pbase + 64, fc * 512: fc * 512 + 512]
                    tx(lambda E, l=lhs, r=rhs, b=b: E.matmul(
                        pb[b][:], l, r, start=True, stop=True),
                       reads=[Rgt_fm, Rxpef], writes=[Rb[b]], inc="T")
                    slot = h % 2
                    pgs = ah[:, PG0 + slot * 512: PG0 + slot * 512 + 512]
                    sx(lambda E, b=b, o=pgs: E.activation(o, pb[b][:], AF.Exp,
                                                          bias=neg8[:], scale=SCALE),
                       reads=[Rb[b], Rconst], writes=[Rpg[slot]], inc="S")
                    for tt in range(4):
                        zout = pb[2][:, qs * 16 + hh * 4 + tt: qs * 16 + hh * 4 + tt + 1]
                        tx(lambda E, l=pgs, o=zout, tt=tt, hh=hh, qs=qs: E.matmul(
                            o, l[:, tt * 128:tt * 128 + 128], ones_b[:, 0:1],
                            start=(hh == 0 and tt == 0), stop=(hh == 3 and tt == 3)),
                           reads=[Rpg[slot], Rconst], writes=[Rb[2]],
                           inc=("T" if (hh == 3 and tt == 3) else None))
                    av_bank = 3 + hh
                    lhs2 = gt_tm[:, h * 64: h * 64 + 64]
                    out = pb[av_bank][pbase:pbase + 64, :]
                    tx(lambda E, l=lhs2, r=pgs, o=out: E.matmul(
                        o, l, r, start=True, stop=True),
                       reads=[Rpg[slot], Rgt_tm], writes=[Rb[av_bank]], inc="T")
                vx(lambda E, qs=qs: E.reciprocal(rztm[:, qs * 16: qs * 16 + 16],
                                                 pb[2][:, qs * 16: qs * 16 + 16]),
                   reads=[Rb[2]], writes=[Rrzt[qs]], inc="V")
                for hh in range(4):
                    h = q * 4 + hh
                    fc, pbase = h // 2, 64 * (h % 2)
                    for kc in range(4):
                        rc = rztm[:, qs * 16 + hh * 4 + kc: qs * 16 + hh * 4 + kc + 1]
                        out = pbb[7][0:1, kc * 128:kc * 128 + 128]
                        tx(lambda E, i=rc, o=out, kc=kc: E.matmul(
                            o, i, id_b[:], start=(kc == 0), stop=(kc == 3),
                            is_transpose=True),
                           reads=[Rrzt[qs], Rconst], writes=[Rb[7]],
                           inc=("T" if kc == 3 else None))
                    rzslot = h % 2
                    rzv = ah[0:1, RZ0 + rzslot * 512: RZ0 + rzslot * 512 + 512]
                    vx(lambda E, o=rzv: E.tensor_copy(o, pbb[7][0:1, 0:512]),
                       reads=[Rb[7]], writes=[Rrz[rzslot]], inc="V")
                    if hh % 2 == 1:
                        rzA = ah[0:1, RZ0: RZ0 + 512]
                        tx(lambda E, r=rzA: E.matmul(pb[7][0:64, :],
                                                     ones_b[0:1, 0:64], r,
                                                     start=True, stop=True),
                           reads=[Rrz[0], Rconst], writes=[Rb[7]], inc=None)
                        tx(lambda E, r=rzv: E.matmul(pb[7][64:128, :],
                                                     ones_b[0:1, 0:64], r,
                                                     start=True, stop=True),
                           reads=[Rrz[1], Rconst], writes=[Rb[7]], inc="T")
                        vx(lambda E: E.tensor_copy(csb[:], pb[7][:]),
                           reads=[Rb[7]], writes=[Rcsb], inc="V")
                        avA, avB = 3 + hh - 1, 3 + hh
                        gA = ah[0:64, GLO + fc * 512: GLO + fc * 512 + 512]
                        gB = ah[64:128, GLO + fc * 512: GLO + fc * 512 + 512]
                        vx(lambda E, o=gA, ab=avA: E.tensor_mul(
                            o, pb[ab][0:64, :], csb[0:64, :]),
                           reads=[Rb[avA], Rcsb], writes=[Rglo], inc="V")
                        vx(lambda E, o=gB, ab=avB: E.tensor_mul(
                            o, pb[ab][64:128, :], csb[64:128, :]),
                           reads=[Rb[avB], Rcsb], writes=[Rglo], inc="V")

            for m in range(NCH):
                bank = m % 3
                for k in range(16):
                    lhs = gw1a[:, k * 1024 + m * 128: k * 1024 + m * 128 + 128]
                    base = LOC if k < 8 else GLO
                    rr = Rloc if k < 8 else Rglo
                    rhs = ah[:, base + (k % 8) * 512: base + (k % 8) * 512 + 512]
                    tx(lambda E, l=lhs, r=rhs, b=bank, k=k: E.matmul(
                        pb[b][:], l, r, start=(k == 0), stop=(k == 15)),
                       reads=[Rgw1, rr], writes=[Rb[bank]],
                       inc=("T" if k == 15 else None))
                msl = ah[:, MIX + m * 512: MIX + m * 512 + 512]
                sx(lambda E, b=bank, o=msl, m=m: E.activation(
                    o, pb[b][:], AF.Sigmoid, bias=gb_s[:, m:m + 1]),
                   reads=[Rb[bank], Rgb], writes=[Rmix], inc="S")
            for fc in range(NCH):
                lsl = ah[:, LOC + fc * 512: LOC + fc * 512 + 512]
                gsl = ah[:, GLO + fc * 512: GLO + fc * 512 + 512]
                msl = ah[:, MIX + fc * 512: MIX + fc * 512 + 512]
                t1 = ah[:, PL0 + (fc % 4) * 512: PL0 + (fc % 4) * 512 + 512]
                vx(lambda E, a=lsl, b=gsl, o=t1: E.tensor_sub(o, a, b),
                   reads=[Rloc, Rglo], writes=[Rpl[fc % 4]], inc="V")
                vx(lambda E, a=t1, b=msl, o=t1: E.tensor_mul(o, a, b),
                   reads=[Rpl[fc % 4], Rmix], writes=[Rpl[fc % 4]], inc="V")
                vx(lambda E, a=t1, b=gsl, o=msl: E.tensor_add(o, a, b),
                   reads=[Rpl[fc % 4], Rglo], writes=[Rmix], inc="V")
            for m in range(NCH):
                bank = 3 + (m % 3)
                for k in range(NCH):
                    lhs = outwa[:, k * 1024 + m * 128: k * 1024 + m * 128 + 128]
                    rhs = ah[:, MIX + k * 512: MIX + k * 512 + 512]
                    tx(lambda E, l=lhs, r=rhs, b=bank, k=k: E.matmul(
                        pb[b][:], l, r, start=(k == 0), stop=(k == 7)),
                       reads=[Routw, Rmix], writes=[Rb[bank]],
                       inc=("T" if k == 7 else None))
                xsl = xy[:, m * 1024 + scb: m * 1024 + scb + 512]
                vx(lambda E, b=bank, o=xsl, m=m: E.scalar_tensor_tensor(
                    o, pb[b][:], ob_s[:, m:m + 1], o, ALU.add, ALU.add),
                   reads=[Rb[bank], Rob, Rxy], writes=[Rxy], inc="V")
            ln_stage(scb, "ln1", t, (6, 7, 0, 1), lambda fc: Rpl[fc % 2])

        alias_edge([Rah], Rpl + Rpg + Rrz + [Rloc, Rglo, Rmix])
        alias_edge(Rl2, [Rxpet])
        alias_edge(Rout, [Rxpef])
        if t == 0:
            dmaw("dgw1", lambda E: E.dma_start(
                out=gw1a.rearrange("p (k c) -> p k c", c=2048), in_=w1_src),
                 writes=[Rgw1])
            wb_store(gw1a.rearrange("p (k c) -> p k c", c=2048), w1b_src, Rgw1, Rsw1)
        else:
            P.emit("gpsimd", lambda E: E.dma_start(
                out=gw1a.rearrange("p (k c) -> p k c", c=2048), in_=w1b_src),
                reads=[Rsw1], writes=[Rgw1], inc="dgw1", k=16)
        if t == 0:
            alias_edge(Rw2, Rconv)

        for sc in range(2):
            scb = sc * 512
            bank_lg = 2
            for fc in range(NCH):
                lhs = rw_s[:, fc * 8: fc * 8 + 8]
                rhs = ynbf[:, fc * 1024 + scb: fc * 1024 + scb + 512]
                tx(lambda E, l=lhs, r=rhs, fc=fc: E.matmul(
                    pb[bank_lg][0:8, :], l, r, start=(fc == 0), stop=(fc == 7)),
                   reads=[Rrw, Rynbf], writes=[Rb[bank_lg]],
                   inc=("T" if fc == 7 else None))
            sx(lambda E: E.activation(ebf[0:8, :], pb[bank_lg][0:8, :], AF.Exp,
                                      bias=rb_s[:]),
               reads=[Rb[bank_lg], Rrb], writes=[Rebf], inc="S")
            bank_et = 3
            for u in range(4):
                tin = ebf[0:8, u * 128: u * 128 + 128]
                out = pb[bank_et][:, u * 8: u * 8 + 8]
                tx(lambda E, i=tin, o=out, u=u: E.matmul(
                    o, i, id_f[0:8, 0:8], start=(u == 0), stop=(u == 3),
                    is_transpose=True),
                   reads=[Rebf, Rconst], writes=[Rb[bank_et]],
                   inc=("T" if u == 3 else None))
            et = pb[bank_et][:, 0:32].rearrange("p (a b) -> p a b", b=8)
            zt = scr4[:, 8:12]
            rzt = scr4[:, 12:16]
            m1 = scr4[:, 0:4]
            m2 = scr4[:, 4:8]
            eq = scrf[:, 0:32]
            dd = scrf[:, 32:64]
            vx(lambda E: E.reduce_sum(zt, et, axis=AX.X), reads=[Rb[bank_et]],
               writes=[Rscr4], inc="V")
            vx(lambda E: E.reciprocal(rzt, zt), reads=[Rscr4], writes=[Rscr4], inc="V")
            vx(lambda E: E.reduce_max(m1, et, axis=AX.X), reads=[Rb[bank_et]],
               writes=[Rscr4], inc="V")
            m1b = m1.rearrange("p (a b) -> p a b", b=1).broadcast_to((128, 4, 8))
            vx(lambda E: E.tensor_tensor(eq.rearrange("p (a b) -> p a b", b=8), et, m1b,
                                         ALU.is_equal),
               reads=[Rb[bank_et], Rscr4], writes=[Rscrf], inc="V")
            vx(lambda E: E.scalar_tensor_tensor(dd, eq, -1e9, et.rearrange("p a b -> p (a b)"),
                                                ALU.mult, ALU.add),
               reads=[Rscrf, Rb[bank_et]], writes=[Rscrf], inc="V")
            vx(lambda E: E.reduce_max(m2, dd.rearrange("p (a b) -> p a b", b=8), axis=AX.X),
               reads=[Rscrf], writes=[Rscr4], inc="V")
            m2b = m2.rearrange("p (a b) -> p a b", b=1).broadcast_to((128, 4, 8))
            vx(lambda E: E.tensor_tensor(eq.rearrange("p (a b) -> p a b", b=8), et, m2b,
                                         ALU.is_ge),
               reads=[Rb[bank_et], Rscr4], writes=[Rscrf], inc="V")
            vx(lambda E: E.tensor_tensor(dd.rearrange("p (a b) -> p a b", b=8),
                                         eq.rearrange("p (a b) -> p a b", b=8), et,
                                         ALU.mult),
               reads=[Rscrf, Rb[bank_et]], writes=[Rscrf], inc="V")
            rztb = rzt.rearrange("p (a b) -> p a b", b=1).broadcast_to((128, 4, 8))
            vx(lambda E: E.tensor_tensor(ctm.rearrange("p (a b) -> p a b", b=8),
                                         dd.rearrange("p (a b) -> p a b", b=8), rztb,
                                         ALU.mult),
               reads=[Rscrf, Rscr4], writes=[Rctm], inc="V")
            bank_cf = 4
            for u in range(4):
                tin = ctm[:, u * 8: u * 8 + 8]
                out = pb[bank_cf][0:8, u * 128: u * 128 + 128]
                tx(lambda E, i=tin, o=out, u=u: E.matmul(
                    o, i, id_f[:], start=(u == 0), stop=(u == 3),
                    is_transpose=True),
                   reads=[Rctm, Rconst], writes=[Rb[bank_cf]],
                   inc=("T" if u == 3 else None))
            vx(lambda E: E.tensor_copy(cfm[0:8, :], pb[bank_cf][0:8, 0:512]),
               reads=[Rb[bank_cf]], writes=[Rcfm], inc="V")

            for er in range(16):
                e = er // 2
                mslot = e % 2
                msb = mwsb[:, mslot * 512: mslot * 512 + 512]
                if er % 2 == 0:
                    mwb_bank = 5 + mslot
                    lhs = basis[0:8, e * 128: e * 128 + 128]
                    tx(lambda E, l=lhs, b=mwb_bank: E.matmul(pb[b][:], l, cfm[0:8, :],
                                                             start=True, stop=True),
                       reads=[Rcfm, Rconst], writes=[Rb[mwb_bank]], inc="T")
                    sx(lambda E, b=mwb_bank, o=msb: E.copy(o, pb[b][:]),
                       reads=[Rb[mwb_bank]], writes=[Rmwsb[mslot]], inc="S")
                bank = er % 3
                for k in range(NCH):
                    lhs = gw1a[:, k * 2048 + er * 128: k * 2048 + er * 128 + 128]
                    rhs = ynbf[:, k * 1024 + scb: k * 1024 + scb + 512]
                    tx(lambda E, l=lhs, r=rhs, b=bank, k=k: E.matmul(
                        pb[b][:], l, r, start=(k == 0), stop=(k == 7)),
                       reads=[Rgw1, Rynbf], writes=[Rb[bank]],
                       inc=("T" if k == 7 else None))
                tsl = tbuf[:, er * 512: er * 512 + 512]
                vx(lambda E, b=bank, o=tsl, m=msb: E.tensor_mul(o, pb[b][:], m),
                   reads=[Rb[bank], Rmwsb[mslot]], writes=[Rtb], inc="V")

            for e8 in range(8):
                reg = e8 % 2
                dst = w2a[:, reg * 8192: reg * 8192 + 8192].rearrange(
                    "p (k c) -> p k c", c=512)
                if t == 0 and sc == 0:
                    dmaw("dw2a" if reg == 0 else "dw2b",
                         lambda E, d=dst, e8=e8: E.dma_start(out=d, in_=w2_src[:, e8]),
                         writes=[Rw2[reg]])
                    wb_store(dst, w2b_src[:, e8], Rw2[reg], Rsw2,
                             "dsw2" if reg == 0 else "dsw2b")
                else:
                    P.emit("gpsimd",
                           lambda E, d=dst, e8=e8: E.dma_start(out=d, in_=w2b_src[:, e8]),
                           reads=[Rsw2], writes=[Rw2[reg]],
                           inc=("dw2a" if reg == 0 else "dw2b"), k=16)
                for f4 in range(4):
                    f = e8 * 4 + f4
                    bank = f % 3
                    for k in range(16):
                        lhs = w2a[:, reg * 8192 + k * 512 + f4 * 128:
                                  reg * 8192 + k * 512 + f4 * 128 + 128]
                        rhs = tbuf[:, k * 512: k * 512 + 512]
                        tx(lambda E, l=lhs, r=rhs, b=bank, k=k: E.matmul(
                            pb[b][:], l, r, start=(k == 0), stop=(k == 15)),
                           reads=[Rw2[reg], Rtb], writes=[Rb[bank]],
                           inc=("T" if k == 15 else None))
                    hsl = ah[:, f * 512: f * 512 + 512]
                    sx(lambda E, b=bank, o=hsl: E.activation(o, pb[b][:], AF.Gelu),
                       reads=[Rb[bank]], writes=[Rah], inc="S")

            if t == 0 and sc == 0:
                Rsw2.writers = {"dsw2": P.cnt["dsw2"], "dsw2b": P.cnt["dsw2b"]}
            for kf in range(32):
                slot = (kf // 2) % 2
                if kf % 2 == 0:
                    dst = xpet[:, slot * 2048: slot * 2048 + 2048].rearrange(
                        "p (k c) -> p k c", c=1024)
                    if t == 0 and sc == 0:
                        dmaw("dl2a" if slot == 0 else "dl2b",
                             lambda E, d=dst, kf=kf: E.dma_start(
                                 out=d, in_=l2_src[:, kf:kf + 2, :]),
                             writes=[Rl2[slot]])
                        wb_store(dst, l2b_src[:, kf:kf + 2, :], Rl2[slot], Rsl2,
                                 "dsl2s" if slot == 0 else "dsl2t")
                    else:
                        P.emit("gpsimd",
                               lambda E, d=dst, kf=kf: E.dma_start(
                                   out=d, in_=l2b_src[:, kf:kf + 2, :]),
                               reads=[Rsl2], writes=[Rl2[slot]],
                               inc=("dl2a" if slot == 0 else "dl2b"), k=16)
                off = (kf % 2) * 1024
                for m in range(NCH):
                    lhs = xpet[:, slot * 2048 + off + m * 128:
                               slot * 2048 + off + m * 128 + 128]
                    rhs = ah[:, kf * 512: kf * 512 + 512]
                    tx(lambda E, l=lhs, r=rhs, b=m, kf=kf: E.matmul(
                        pb[b][:], l, r, start=(kf == 0), stop=(kf == 31)),
                       reads=[Rl2[slot], Rah], writes=[Rb[m]],
                       inc=("T" if (kf % 2 == 1 and m == NCH - 1) else None))
            if t == 0 and sc == 0:
                Rsl2.writers = {"dsl2s": P.cnt["dsl2s"], "dsl2t": P.cnt["dsl2t"]}
            for m in range(NCH):
                ysl = ynbf[:, m * 1024 + scb: m * 1024 + scb + 512]
                vx(lambda E, b=m, y=ysl, m=m: E.scalar_tensor_tensor(
                    y, pb[b][:], lb_s[:, m:m + 1], y, ALU.add, ALU.add),
                   reads=[Rb[m], Rlb, Rynbf], writes=[Rynbf], inc="V")
            ln_stage(scb, "ln2", t, (0, 1, 2, 3), lambda fc: Rah)

        if t + 1 < NT:
            P.emit("gpsimd", lambda E: E.dma_start(
                out=gw1a.rearrange("p (k c) -> p k c", c=1024), in_=gwb_src),
                reads=[Rsgw], writes=[Rgw1], inc="dgw1", k=16)
            P.emit("gpsimd", lambda E: E.dma_start(
                out=outwa.rearrange("p (k c) -> p k c", c=1024), in_=owb_src),
                reads=[Rsow], writes=[Routw], inc="doutw", k=16)

    P.raw_wait("sync", {"dout0": P.cnt["dout0"], "dout1": P.cnt["dout1"]})

    semh = {}
    for name in ("T", "V", "S", "G") + DMA_SEMS:
        semh[name] = stack.enter_context(nc.semaphore(name))

    ctx_lp = nc.allow_low_precision(
        reason="bf16 operands for matmuls are intended; fp32 PSUM accumulation")
    ctx_lp.__enter__()
    with nc.Block() as block:
        def run_queue(qname):
            def runner(E):
                for item in P.q[qname]:
                    if item[0] == "w":
                        E.wait_ge(semh[item[1]], item[2])
                    else:
                        _, fn, inc, k = item
                        ins = fn(E)
                        if inc is not None:
                            ins.then_inc(semh[inc], k)
            return runner

        block.tensor(run_queue("tensor"))
        block.vector(run_queue("vector"))
        block.scalar(run_queue("scalar"))
        block.sync(run_queue("sync"))
        block.gpsimd(run_queue("gpsimd"))

    ctx_lp.__exit__(None, None, None)
    stack.close()
    return nc


_NC_CACHE = None


def kernel(L0_attn_x, L0_attn_pos_emb, L0_attn_global_memory, L0_attn_conv_w,
           L0_attn_conv_b, L0_attn_gate_w, L0_attn_gate_b, L0_attn_out_w,
           L0_attn_out_b, L0_ffn_router_w, L0_ffn_router_b, L0_ffn_w1,
           L0_ffn_w2, L0_ffn_lin2_w, L0_ffn_lin2_b):
    global _NC_CACHE
    if _NC_CACHE is None:
        _NC_CACHE = build_nc()
    nc = _NC_CACHE

    f32 = np.float32
    ac = np.ascontiguousarray
    pos = ac(L0_attn_pos_emb[:S], dtype=f32)
    pos_f = ac(pos.T)
    convw = np.concatenate(
        [L0_attn_conv_w.transpose(2, 1, 0).reshape(4096, D),
         L0_attn_conv_b.reshape(1, D)], axis=0)
    convw = ac(convw, dtype=f32)
    gate_w = ac(L0_attn_gate_w, dtype=f32)
    out_w = ac(L0_attn_out_w, dtype=f32)
    w1fh = ac(L0_ffn_w1.transpose(1, 0, 2).reshape(D, 2048), dtype=f32)
    w2fh = ac(L0_ffn_w2.reshape(2048, FF), dtype=f32)
    lin2 = ac(L0_ffn_lin2_w, dtype=f32)
    gb = ac(L0_attn_gate_b.reshape(8, 128).T, dtype=f32)
    ob = ac(L0_attn_out_b.reshape(8, 128).T, dtype=f32)
    lb = ac(L0_ffn_lin2_b.reshape(8, 128).T, dtype=f32)
    rw = ac(L0_ffn_router_w.reshape(8, 128, 8).transpose(1, 0, 2).reshape(128, 64),
            dtype=f32)
    rb = ac(L0_ffn_router_b.reshape(8, 1), dtype=f32)
    gm = ac(L0_attn_global_memory[0], dtype=f32)

    in_maps = []
    for b in range(8):
        xb = ac(L0_attn_x[b], dtype=f32)
        in_maps.append({
            "x_tm": xb, "x_fm": ac(xb.T),
            "pos_tm": pos, "pos_fm": pos_f,
            "gmem": gm, "convw": convw,
            "gate_w": gate_w, "out_w": out_w,
            "w1f": w1fh, "w2f": w2fh, "lin2_w": lin2,
            "gate_b": gb, "out_b": ob, "lin2_b": lb,
            "router_w": rw, "router_b": rb,
        })
    res = run_bass_kernel_spmd(nc, in_maps, core_ids=list(range(8)))
    out = np.empty((8, S, D), np.float32)
    for b in range(8):
        out[b] = res.results[b]["out_fm"].T
    return out


# version 12
# speedup vs baseline: 1.1814x; 1.0243x over previous
import numpy as np
import concourse.bass as bass
import concourse.mybir as mybir
from concourse.bass_utils import run_bass_kernel_spmd

F32 = mybir.dt.float32
BF16 = mybir.dt.bfloat16
AF = mybir.ActivationFunctionType
ALU = mybir.AluOpType
AX = mybir.AxisListType

S, D, H, HD, WIN = 4096, 1024, 16, 64, 512
TILE, NT, SC = 1024, 4, 512
NCH = 8
E8, RR, FF = 8, 256, 4096
SCALE = 0.125
CEXP = -8.0

ENGINES = ("tensor", "vector", "scalar", "sync", "gpsimd")


class Reg:
    __slots__ = ("writers", "readers")

    def __init__(self):
        self.writers = {}
        self.readers = {}

    def wrote(self, gate):
        s, v = gate
        self.writers = {s: v}
        self.readers = {}

    def read(self, gate):
        s, v = gate
        if self.readers.get(s, 0) < v:
            self.readers[s] = v

def alias_edge(dsts, srcs):
    merged = {}
    for o in srcs:
        for d in (o.writers, o.readers):
            for s, v in d.items():
                if merged.get(s, 0) < v:
                    merged[s] = v
    for o in srcs:
        o.writers = {}
        o.readers = {}
    for r in dsts:
        r.writers = {}
        r.readers = dict(merged)


class Prog:
    def __init__(self):
        self.q = {e: [] for e in ENGINES}
        self.cnt = {}
        self.guar = {e: {} for e in ENGINES}
        self.pend = {e: [] for e in ENGINES}

    def sem(self, name):
        self.cnt[name] = 0
        return name

    def _wait(self, eng, sname, val):
        if val <= 0:
            return
        if self.guar[eng].get(sname, 0) >= val:
            return
        self.guar[eng][sname] = val
        self.q[eng].append(("w", sname, val))

    def emit(self, eng, fn, reads=(), writes=(), inc=None, k=1):
        need = {}
        for r in reads:
            for s, v in r.writers.items():
                if need.get(s, 0) < v:
                    need[s] = v
        for r in writes:
            for d in (r.writers, r.readers):
                for s, v in d.items():
                    if need.get(s, 0) < v:
                        need[s] = v
        if inc is not None and inc[0] == "d":
            need.pop(inc, None)
        for s, v in need.items():
            self._wait(eng, s, v)
        if inc is None:
            self.q[eng].append(("op", fn, None, 0))
            self.pend[eng].append((tuple(reads), tuple(writes)))
            return None
        self.cnt[inc] += k
        gate = (inc, self.cnt[inc])
        if eng == "tensor" and inc == "T":
            self.guar[eng][inc] = self.cnt[inc]
        self.q[eng].append(("op", fn, inc, k))
        for rr, ww in self.pend[eng]:
            for r in rr:
                r.read(gate)
            for w in ww:
                w.wrote(gate)
        self.pend[eng].clear()
        for r in reads:
            r.read(gate)
        for r in writes:
            r.wrote(gate)
        return gate

    def raw_wait(self, eng, gatedict):
        for s, v in gatedict.items():
            self._wait(eng, s, v)


def build_nc():
    nc = bass.Bass("TRN2", target_bir_lowering=False, debug=False)
    P = Prog()
    for s in ("T", "V", "S", "G"):
        P.sem(s)
    DMA_SEMS = ("dxy", "dsl0", "dsl1", "dpf0", "dpf1", "dgw1", "doutw",
                "dw2a", "dw2b", "dl2a", "dl2b", "dmisc", "dbias",
                "dout0", "dout1", "dsb", "dsw2", "dsw2b", "dsl2s", "dsl2t")
    for s in DMA_SEMS:
        P.sem(s)

    dt_in = lambda n, sh, dt=F32: nc.dram_tensor(n, sh, dt, kind="ExternalInput").ap()
    x_tm = dt_in("x_tm", [S, D])
    x_fm = dt_in("x_fm", [D, S])
    pos_tm = dt_in("pos_tm", [S, D])
    pos_fm = dt_in("pos_fm", [D, S])
    gmem = dt_in("gmem", [64, D])
    convw = dt_in("convw", [4097, D])
    gate_w = dt_in("gate_w", [2048, D])
    out_w = dt_in("out_w", [D, D])
    w1f = dt_in("w1f", [D, 2048])
    w2f = dt_in("w2f", [2048, FF])
    lin2_w = dt_in("lin2_w", [FF, D])
    gate_b = dt_in("gate_b", [128, 8])
    out_b = dt_in("out_b", [128, 8])
    lin2_b = dt_in("lin2_b", [128, 8])
    router_w = dt_in("router_w", [128, 64])
    router_b = dt_in("router_b", [8, 1])
    out_fm = nc.dram_tensor("out_fm", [D, S], F32, kind="ExternalOutput").ap()
    dt_sc = lambda n, sh: nc.dram_tensor(n, sh, BF16, kind="Internal").ap()
    gw_bf = dt_sc("gw_bf", [2048, D])
    ow_bf = dt_sc("ow_bf", [D, D])
    w1_bf = dt_sc("w1_bf", [D, 2048])
    w2_bf = dt_sc("w2_bf", [2048, FF])
    l2_bf = dt_sc("l2_bf", [FF, D])

    import contextlib
    stack = contextlib.ExitStack()
    sb = lambda n, sh, dt: stack.enter_context(nc.sbuf_tensor(n, sh, dt)).ap()
    ps = lambda n, sh, dt: stack.enter_context(nc.psum_tensor(n, sh, dt)).ap()

    xy = sb("xy", [128, 8192], BF16)
    ynbf = sb("ynbf", [128, 8192], BF16)
    xpef = sb("xpef", [128, 4096], BF16)
    xpet = sb("xpet", [128, 4096], BF16)
    ah = sb("ah", [128, 16384], BF16)
    tbuf = sb("tbuf", [128, 8192], BF16)
    gw1a = sb("gw1a", [128, 16384], BF16)
    outwa = sb("outwa", [128, 8192], BF16)
    w2a = sb("w2a", [128, 16384], BF16)
    gt_tm = sb("gt_tm", [128, 1024], BF16)

    gt_fm = sb("gt_fm", [128, 1024], BF16)
    ebf = sb("ebf", [128, 512], F32)
    cfm = sb("cfm", [128, 512], BF16)
    ctm = sb("ctm", [128, 32], F32)
    csb = sb("csb", [128, 512], BF16)
    mwsb = sb("mwsb", [128, 1024], BF16)
    ztm = sb("ztm", [128, 32], F32)
    rztm = sb("rztm", [128, 32], BF16)
    scr4 = sb("scr4", [128, 16], F32)
    scrf = sb("scrf", [128, 1024], F32)
    gt_tf = scrf
    statv = sb("statv", [1, 2048], F32)
    ones_b = sb("ones_b", [128, 128], BF16)
    ones_f = sb("ones_f", [128, 128], F32)
    id_f = sb("id_f", [128, 128], F32)
    id_b = sb("id_b", [128, 128], BF16)
    basis = sb("basis", [8, 1024], BF16)
    neg8 = sb("neg8", [128, 1], F32)
    epsc = sb("epsc", [128, 1], F32)
    gb_s = sb("gb_s", [128, 8], F32)
    ob_s = sb("ob_s", [128, 8], F32)
    lb_s = sb("lb_s", [128, 8], F32)
    rb_s = sb("rb_s", [8, 1], F32)
    rw_s = sb("rw_s", [128, 64], BF16)
    cbrow = sb("cbrow", [1, 1024], BF16)
    iot = sb("iot", [128, 1], F32)

    pb = [ps(f"pb{i}", [128, 512], F32) for i in range(8)]
    pbb = [p.bitcast(BF16) for p in pb]

    LOC, GLO, MIX, PL0, PG0, RZ0 = 0, 4096, 8192, 12288, 14336, 15360

    R = lambda: Reg()
    Rxy, Rynbf, Rxpef, Rxpet = R(), R(), R(), R()
    Rloc, Rglo, Rmix = R(), R(), R()
    Rpl = [R() for _ in range(4)]
    Rpg = [R() for _ in range(2)]
    Rrz = [R() for _ in range(2)]
    Rah = R()
    Rtb = R()
    Rslab = [R() for _ in range(6)]
    Rgw1, Routw = R(), R()
    Rw2 = [R() for _ in range(2)]
    Rconv = [R() for _ in range(4)]
    Rl2 = [R() for _ in range(4)]
    Rout = [R() for _ in range(2)]
    Rgt_tm, Rgt_fm, Rgt_tf = R(), R(), R()
    Rgb, Rob, Rlb, Rrb, Rrw, Rcbrow = R(), R(), R(), R(), R(), R()
    Rsgw, Rsow, Rsw1, Rsw2, Rsl2 = R(), R(), R(), R(), R()
    Rebf, Rcfm, Rctm, Rscr4, Rscrf, Rstat = R(), R(), R(), R(), R(), R()
    Rzt = [R(), R()]
    Rrzt = [R(), R()]
    Rcsb = R()
    Rmwsb = [R(), R()]
    Rb = [R() for _ in range(8)]
    Rconst = R()
    Rbias = R()

    outst = xpef.bitcast(F32)

    def gp(fn, **kw):
        return P.emit("gpsimd", fn, **kw)

    def vx(fn, **kw):
        return P.emit("vector", fn, **kw)

    def sx(fn, **kw):
        return P.emit("scalar", fn, **kw)

    def tx(fn, **kw):
        return P.emit("tensor", fn, **kw)

    def dmaw(sem, fn, **kw):
        return P.emit("gpsimd", fn, inc=sem, k=16, **kw)

    def dmax(sem, fn, **kw):
        return P.emit("sync", fn, inc=sem, k=16, **kw)

    gp(lambda E: E.memset(ones_b[:], 1.0), writes=[Rconst], inc="G")
    gp(lambda E: E.memset(ones_f[:], 1.0), writes=[Rconst], inc="G")
    gp(lambda E: E.memset(neg8[:], CEXP), writes=[Rconst], inc="G")
    gp(lambda E: E.memset(epsc[:], 1e-5), writes=[Rconst], inc="G")
    gp(lambda E: E.iota(id_f[:], pattern=[[1, 128]], channel_multiplier=0,
                        allow_small_or_imprecise_dtypes=True),
       writes=[Rconst], inc="G")
    gp(lambda E: E.iota(iot[:], pattern=[[0, 1]], channel_multiplier=1,
                        allow_small_or_imprecise_dtypes=True),
       writes=[Rconst], inc="G")
    vx(lambda E: E.tensor_scalar(id_f[:], id_f[:], iot[:], None, ALU.is_equal),
       reads=[Rconst], writes=[Rconst], inc="V")
    vx(lambda E: E.tensor_copy(id_b[:], id_f[:]), reads=[Rconst], writes=[Rconst], inc="V")
    gp(lambda E: E.iota(basis[:], pattern=[[1, 8], [0, 128]], channel_multiplier=0,
                        allow_small_or_imprecise_dtypes=True),
       writes=[Rconst], inc="G")
    vx(lambda E: E.tensor_scalar(basis[:], basis[:], iot[0:8, :], None, ALU.is_equal),
       reads=[Rconst], writes=[Rconst], inc="V")
    dmax("dbias", lambda E: E.dma_start(out=gb_s[:], in_=gate_b), writes=[Rgb])
    dmax("dbias", lambda E: E.dma_start(out=ob_s[:], in_=out_b), writes=[Rob])
    dmax("dbias", lambda E: E.dma_start(out=lb_s[:], in_=lin2_b), writes=[Rlb])
    dmax("dbias", lambda E: E.dma_start(out=rb_s[:], in_=router_b), writes=[Rrb])
    dmaw("dmisc", lambda E: E.dma_start(out=rw_s[:], in_=router_w), writes=[Rrw])
    dmaw("dmisc", lambda E: E.dma_start(out=gt_tm[64:128, :], in_=gmem), writes=[Rgt_tm])
    dmax("dbias", lambda E: E.dma_start(out=gt_tf[64:128, :], in_=gmem), writes=[Rgt_tf])
    dmaw("dmisc", lambda E: E.dma_start(out=cbrow[:], in_=convw[4096:4097, :]), writes=[Rcbrow])
    conv_src = convw[0:4096, :].rearrange("(j p) c -> p j c", p=128)

    def conv_group_load(g):
        slot = g % 2
        dst = w2a[:, slot * 4096:(slot + 1) * 4096].rearrange("p (j c) -> p j c", c=1024)
        dmaw("dw2a" if slot == 0 else "dw2b",
             lambda E, g=g, d=dst: E.dma_start(out=d, in_=conv_src[:, 4 * g:4 * g + 4, :]),
             writes=[Rconv[slot]])

    for r in (Rgb, Rob, Rlb, Rrb, Rgt_tf):
        r.writers = {"dbias": P.cnt["dbias"]}
    for r in (Rrw, Rgt_tm, Rcbrow):
        r.writers = {"dmisc": P.cnt["dmisc"]}

    conv_group_load(0)
    conv_group_load(1)
    gw_src = gate_w.rearrange("(k p) c -> p k c", p=128)
    w1_src = w1f.rearrange("(k p) c -> p k c", p=128)
    l2_src = lin2_w.rearrange("(k p) c -> p k c", p=128)
    w2_src = w2f.rearrange("(k p) (e c) -> p e k c", p=128, c=512)
    gwb_src = gw_bf.rearrange("(k p) c -> p k c", p=128)
    owb_src = ow_bf.rearrange("(k p) c -> p k c", p=128)
    w1b_src = w1_bf.rearrange("(k p) c -> p k c", p=128)
    l2b_src = l2_bf.rearrange("(k p) c -> p k c", p=128)
    w2b_src = w2_bf.rearrange("(k p) (e c) -> p e k c", p=128, c=512)

    def wb_store(sbuf_view, dst_view, slab_reg, scr_reg, sem="dsb"):
        P.emit("sync", lambda E, s=sbuf_view, d=dst_view: E.dma_start(out=d, in_=s),
               reads=[slab_reg], writes=[scr_reg], inc=sem, k=16)

    dmaw("dgw1", lambda E: E.dma_start(out=gw1a.rearrange("p (k c) -> p k c", c=1024),
                               in_=gw_src),
         writes=[Rgw1])
    wb_store(gw1a.rearrange("p (k c) -> p k c", c=1024), gwb_src, Rgw1, Rsgw)
    ow_src = out_w.rearrange("(k p) c -> p k c", p=128)
    dmaw("doutw", lambda E: E.dma_start(out=outwa.rearrange("p (k c) -> p k c", c=1024),
                                in_=ow_src),
         writes=[Routw])
    wb_store(outwa.rearrange("p (k c) -> p k c", c=1024), owb_src, Routw, Rsow)
    for r in (Rsgw, Rsow):
        r.writers = {"dsb": P.cnt["dsb"]}
    Rgw1.readers["dsb"] = P.cnt["dsb"]
    Routw.readers["dsb"] = P.cnt["dsb"]


    def ln_stage(sc_base, out_kind, t, bank_ids, scratch_reg):
        bsy, bsq, bb0, bb1 = bank_ids
        srct, srcr = (xy, Rxy) if out_kind == "ln1" else (ynbf, Rynbf)
        for fc in range(NCH):
            xsl = srct[:, fc * 1024 + sc_base: fc * 1024 + sc_base + 512]
            pl = ah[:, PL0 + (fc % 2) * 512: PL0 + (fc % 2) * 512 + 512]
            srg = scratch_reg(fc)
            sx(lambda E, i=xsl, o=pl: E.activation(o, i, AF.Square),
               reads=[srcr], writes=[srg], inc="S")
            tx(lambda E, r=xsl, fc=fc: E.matmul(pb[bsy][0:1, :], ones_b[:, 0:1], r,
                                                start=(fc == 0), stop=(fc == 7)),
               reads=[srcr, Rconst], writes=[Rb[bsy]], inc="T")
            tx(lambda E, r=pl, fc=fc: E.matmul(pb[bsq][0:1, :], ones_b[:, 0:1], r,
                                               start=(fc == 0), stop=(fc == 7)),
               reads=[srg, Rconst], writes=[Rb[bsq]], inc="T")
        mean = statv[0:1, 0:512]
        inv = statv[0:1, 512:1024]
        minv = statv[0:1, 1024:1536]
        var = statv[0:1, 1536:2048]
        vx(lambda E: E.tensor_scalar(mean, pb[bsy][0:1, :], 1.0 / 1024, None, ALU.mult),
           reads=[Rb[bsy]], writes=[Rstat], inc="V")
        vx(lambda E: E.tensor_mul(var, mean, mean), reads=[Rstat], writes=[Rstat], inc="V")
        vx(lambda E: E.scalar_tensor_tensor(var, pb[bsq][0:1, :], 1.0 / 1024, var,
                                            ALU.mult, ALU.subtract),
           reads=[Rb[bsq], Rstat], writes=[Rstat], inc="V")
        sx(lambda E: E.activation(var, var, AF.Ln, bias=epsc[0:1, :]),
           reads=[Rstat, Rconst], writes=[Rstat], inc="S")
        sx(lambda E: E.activation(inv, var, AF.Exp, scale=-0.5),
           reads=[Rstat], writes=[Rstat], inc="S")
        vx(lambda E: E.tensor_mul(minv, mean, inv), reads=[Rstat], writes=[Rstat], inc="V")
        invb = statv.bitcast(BF16)[0:1, 3072:3584]
        minvb = statv.bitcast(BF16)[0:1, 3584:4096]
        vx(lambda E: E.tensor_copy(invb, inv), reads=[Rstat], writes=[Rstat], inc="V")
        vx(lambda E: E.tensor_copy(minvb, minv), reads=[Rstat], writes=[Rstat], inc="V")
        tx(lambda E: E.matmul(pb[bb0][:], ones_b[0:1, :], invb, start=True, stop=True),
           reads=[Rstat, Rconst], writes=[Rb[bb0]], inc="T")
        tx(lambda E: E.matmul(pb[bb1][:], ones_b[0:1, :], minvb, start=True, stop=True),
           reads=[Rstat, Rconst], writes=[Rb[bb1]], inc="T")
        for fc in range(NCH):
            xsl = srct[:, fc * 1024 + sc_base: fc * 1024 + sc_base + 512]
            q1 = scrf[:, (fc % 2) * 512:(fc % 2) * 512 + 512]
            vx(lambda E, i=xsl, o=q1: E.tensor_mul(o, i, pb[bb0][:]),
               reads=[srcr, Rb[bb0]], writes=[Rscrf], inc="V")
            if out_kind == "ln1":
                osl = ynbf[:, fc * 1024 + sc_base: fc * 1024 + sc_base + 512]
                vx(lambda E, i=q1, o=osl: E.tensor_sub(o, i, pb[bb1][:]),
                   reads=[Rscrf, Rb[bb1]], writes=[Rynbf], inc="V")
            else:
                slot = fc % 2
                osl = outst[:, slot * 512: slot * 512 + 512]
                vx(lambda E, i=q1, o=osl: E.tensor_sub(o, i, pb[bb1][:]),
                   reads=[Rscrf, Rb[bb1]], writes=[Rout[slot]], inc="V")
                dst = out_fm[fc * 128:(fc + 1) * 128,
                             t * 1024 + sc_base: t * 1024 + sc_base + 512]
                P.emit("sync", lambda E, s=osl, d=dst: E.dma_start(out=d, in_=s),
                       reads=[Rout[slot]], inc="dout%d" % slot, k=16)

    for t in range(NT):
        tb0 = t * 1024
        if t > 0:
            alias_edge([Rxpet], Rl2)
            alias_edge([Rxpef], Rout)
            alias_edge(Rpl + Rpg + Rrz + [Rloc, Rglo, Rmix], [Rah])
        xfm_g = x_fm.rearrange("(f p) s -> p f s", p=128)
        pfm_g = pos_fm.rearrange("(f p) s -> p f s", p=128)
        xtm_g = x_tm.rearrange("(g p) d -> p g d", p=128)
        ptm_g = pos_tm.rearrange("(g p) d -> p g d", p=128)
        for sc in range(2):
            scb = sc * 512
            dmaw("dpf0", lambda E, s=xfm_g[:, :, tb0 + scb: tb0 + scb + 512],
                 d=xpef.rearrange("p (f s) -> p f s", s=512): E.dma_start(out=d, in_=s),
                 writes=[Rxpef])
            dmaw("dpf1", lambda E, s=pfm_g[:, :, tb0 + scb: tb0 + scb + 512],
                 d=xpef.rearrange("p (f s) -> p f s", s=512): E.dma_start(
                     out=d, in_=s, accum_op=ALU.add),
                 writes=[Rxpef])
            g0 = (tb0 + scb) // 128
            dmaw("dsl0", lambda E, s=xtm_g[:, g0:g0 + 4, :],
                 d=xpet.rearrange("p (g d) -> p g d", d=1024): E.dma_start(out=d, in_=s),
                 writes=[Rxpet])
            dmaw("dsl1", lambda E, s=ptm_g[:, g0:g0 + 4, :],
                 d=xpet.rearrange("p (g d) -> p g d", d=1024): E.dma_start(
                     out=d, in_=s, accum_op=ALU.add),
                 writes=[Rxpet])
            if sc == 0:
                for fcx in range(NCH):
                    srcx = x_fm[fcx * 128:(fcx + 1) * 128, tb0:tb0 + 1024]
                    dstx = xy[:, fcx * 1024: fcx * 1024 + 1024]
                    dmaw("dxy", lambda E, s=srcx, d=dstx: E.dma_start(out=d, in_=s),
                         writes=[Rxy])

            scnt = 0
            pcnt = 0
            for q in range(4):
                qs = q % 2
                for hp in range(2):
                    hA = q * 4 + hp * 2
                    fc = hA // 2
                    pair0 = pcnt
                    def emit_av(hh2, pbase, kc):
                        h = q * 4 + hh2
                        av_bank = 3 + hh2
                        slot = (pair0 + kc * 2 + (hh2 - hp * 2)) % 4
                        pls = ah[:, PL0 + slot * 512: PL0 + slot * 512 + 512]
                        lhs = xpet[:, kc * 1024 + 64 * h:
                                   kc * 1024 + 64 * h + 64]
                        out = pb[av_bank][pbase:pbase + 64, :]
                        tx(lambda E, l=lhs, r=pls, o=out, kc=kc: E.matmul(
                            o, l, r, start=(kc == 0), stop=(kc == 3)),
                           reads=[Rpl[slot], Rxpet], writes=[Rb[av_bank]],
                           inc=("T" if kc == 3 else None))

                    for kc in range(4):
                        for hh2, pbase in ((hp * 2, 0), (hp * 2 + 1, 64)):
                            h = q * 4 + hh2
                            b = scnt % 3
                            scnt += 1
                            lhs = xpef[pbase:pbase + 64,
                                       fc * 512 + kc * 128:
                                       fc * 512 + kc * 128 + 128]
                            rhs = xpef[pbase:pbase + 64,
                                       fc * 512: fc * 512 + 512]
                            tx(lambda E, l=lhs, r=rhs, b=b: E.matmul(
                                pb[b][:], l, r, start=True, stop=True),
                               reads=[Rxpef], writes=[Rb[b]], inc="T")
                            slot = pcnt % 4
                            pcnt += 1
                            pls = ah[:, PL0 + slot * 512: PL0 + slot * 512 + 512]
                            zc = ztm[:, qs * 16 + hh2 * 4 + kc:
                                     qs * 16 + hh2 * 4 + kc + 1]
                            sx(lambda E, b=b, o=pls, z=zc: E.activation(
                                o, pb[b][:], AF.Exp, bias=neg8[:], scale=SCALE,
                                accum_out=z),
                               reads=[Rb[b], Rconst], writes=[Rpl[slot], Rzt[qs]],
                               inc="S")
                        if kc >= 1:
                            emit_av(hp * 2, 0, kc - 1)
                            emit_av(hp * 2 + 1, 64, kc - 1)
                    emit_av(hp * 2, 0, 3)
                    emit_av(hp * 2 + 1, 64, 3)
                vx(lambda E, qs=qs: E.reciprocal(rztm[:, qs * 16: qs * 16 + 16],
                                                 ztm[:, qs * 16: qs * 16 + 16]),
                   reads=[Rzt[qs]], writes=[Rrzt[qs]], inc="V")
                for hh in range(4):
                    h = q * 4 + hh
                    fc, pbase = h // 2, 64 * (h % 2)
                    for kc in range(4):
                        rc = rztm[:, qs * 16 + hh * 4 + kc: qs * 16 + hh * 4 + kc + 1]
                        out = pbb[7][0:1, kc * 128:kc * 128 + 128]
                        tx(lambda E, i=rc, o=out, kc=kc: E.matmul(
                            o, i, id_b[:], start=(kc == 0), stop=(kc == 3),
                            is_transpose=True),
                           reads=[Rrzt[qs], Rconst], writes=[Rb[7]],
                           inc=("T" if kc == 3 else None))
                    rzslot = h % 2
                    rzv = ah[0:1, RZ0 + rzslot * 512: RZ0 + rzslot * 512 + 512]
                    vx(lambda E, o=rzv: E.tensor_copy(o, pbb[7][0:1, 0:512]),
                       reads=[Rb[7]], writes=[Rrz[rzslot]], inc="V")
                    if hh % 2 == 1:
                        rzA = ah[0:1, RZ0: RZ0 + 512]
                        tx(lambda E, r=rzA: E.matmul(pb[7][0:64, :],
                                                     ones_b[0:1, 0:64], r,
                                                     start=True, stop=True),
                           reads=[Rrz[0], Rconst], writes=[Rb[7]], inc=None)
                        tx(lambda E, r=rzv: E.matmul(pb[7][64:128, :],
                                                     ones_b[0:1, 0:64], r,
                                                     start=True, stop=True),
                           reads=[Rrz[1], Rconst], writes=[Rb[7]], inc="T")
                        vx(lambda E: E.tensor_copy(csb[:], pb[7][:]),
                           reads=[Rb[7]], writes=[Rcsb], inc="V")
                        avA, avB = 3 + hh - 1, 3 + hh
                        lA = ah[0:64, LOC + fc * 512: LOC + fc * 512 + 512]
                        lB = ah[64:128, LOC + fc * 512: LOC + fc * 512 + 512]
                        vx(lambda E, o=lA, ab=avA: E.tensor_mul(
                            o, pb[ab][0:64, :], csb[0:64, :]),
                           reads=[Rb[avA], Rcsb], writes=[Rloc], inc="V")
                        vx(lambda E, o=lB, ab=avB: E.tensor_mul(
                            o, pb[ab][64:128, :], csb[64:128, :]),
                           reads=[Rb[avB], Rcsb], writes=[Rloc], inc="V")

            if t == 0 and sc == 0:
                for j in range(32):
                    kc, ic = j // 8, j % 8
                    g, jj = j // 4, j % 4
                    slot = g % 2
                    if jj == 0 and g >= 2:
                        conv_group_load(g)
                    lhs = xpef[:, ic * 512: ic * 512 + 256].rearrange(
                        "p (t f) -> p t f", f=4)[:, :, kc]
                    for oh in range(2):
                        bank = 3 + oh
                        rhs = w2a[:, slot * 4096 + jj * 1024 + oh * 512:
                                  slot * 4096 + jj * 1024 + oh * 512 + 512]
                        tx(lambda E, l=lhs, r=rhs, b=bank, j=j: E.matmul(
                            pb[b][0:64, :], l, r, start=(j == 0), stop=False),
                           reads=[Rxpef, Rconv[slot]], writes=[Rb[bank]],
                           inc=("T" if (j % 4) == 3 and oh == 1 else None))
                for oh in range(2):
                    bank = 3 + oh
                    rhsb = cbrow[0:1, oh * 512: oh * 512 + 512]
                    tx(lambda E, r=rhsb, b=bank: E.matmul(
                        pb[b][0:64, :], ones_b[0:1, 0:64], r, start=False, stop=True),
                       reads=[Rcbrow, Rconst], writes=[Rb[bank]], inc="T")
                    vx(lambda E, b=bank, oh=oh: E.tensor_copy(
                        gt_tm[0:64, oh * 512: oh * 512 + 512], pb[b][0:64, :]),
                       reads=[Rb[bank]], writes=[Rgt_tm], inc="V")
                    vx(lambda E, b=bank, oh=oh: E.tensor_copy(
                        gt_tf[0:64, oh * 512: oh * 512 + 512], pb[b][0:64, :]),
                       reads=[Rb[bank]], writes=[Rgt_tf], inc="V")
                for fc in range(NCH):
                    bank = 5
                    tin = gt_tf[:, fc * 128: fc * 128 + 128]
                    tx(lambda E, i=tin, b=bank: E.transpose(pb[b][:, 0:128], i, id_f[:]),
                       reads=[Rgt_tf, Rconst], writes=[Rb[bank]], inc="T")
                    vx(lambda E, fc=fc, b=bank: E.tensor_copy(
                        gt_fm[:, fc * 128: fc * 128 + 128], pb[b][:, 0:128]),
                       reads=[Rb[bank]], writes=[Rgt_fm], inc="V")
                alias_edge([Rscrf], [Rgt_tf])

            scnt = 0
            for q in range(4):
                qs = q % 2
                for hh in range(4):
                    h = q * 4 + hh
                    fc, pbase = h // 2, 64 * (h % 2)
                    b = scnt % 2
                    scnt += 1
                    lhs = gt_fm[pbase:pbase + 64, fc * 128: fc * 128 + 128]
                    rhs = xpef[pbase:pbase + 64, fc * 512: fc * 512 + 512]
                    tx(lambda E, l=lhs, r=rhs, b=b: E.matmul(
                        pb[b][:], l, r, start=True, stop=True),
                       reads=[Rgt_fm, Rxpef], writes=[Rb[b]], inc="T")
                    slot = h % 2
                    pgs = ah[:, PG0 + slot * 512: PG0 + slot * 512 + 512]
                    sx(lambda E, b=b, o=pgs: E.activation(o, pb[b][:], AF.Exp,
                                                          bias=neg8[:], scale=SCALE),
                       reads=[Rb[b], Rconst], writes=[Rpg[slot]], inc="S")
                    for tt in range(4):
                        zout = pb[2][:, qs * 16 + hh * 4 + tt: qs * 16 + hh * 4 + tt + 1]
                        tx(lambda E, l=pgs, o=zout, tt=tt, hh=hh, qs=qs: E.matmul(
                            o, l[:, tt * 128:tt * 128 + 128], ones_b[:, 0:1],
                            start=(hh == 0 and tt == 0), stop=(hh == 3 and tt == 3)),
                           reads=[Rpg[slot], Rconst], writes=[Rb[2]],
                           inc=("T" if (hh == 3 and tt == 3) else None))
                    av_bank = 3 + hh
                    lhs2 = gt_tm[:, h * 64: h * 64 + 64]
                    out = pb[av_bank][pbase:pbase + 64, :]
                    tx(lambda E, l=lhs2, r=pgs, o=out: E.matmul(
                        o, l, r, start=True, stop=True),
                       reads=[Rpg[slot], Rgt_tm], writes=[Rb[av_bank]], inc="T")
                vx(lambda E, qs=qs: E.reciprocal(rztm[:, qs * 16: qs * 16 + 16],
                                                 pb[2][:, qs * 16: qs * 16 + 16]),
                   reads=[Rb[2]], writes=[Rrzt[qs]], inc="V")
                for hh in range(4):
                    h = q * 4 + hh
                    fc, pbase = h // 2, 64 * (h % 2)
                    for kc in range(4):
                        rc = rztm[:, qs * 16 + hh * 4 + kc: qs * 16 + hh * 4 + kc + 1]
                        out = pbb[7][0:1, kc * 128:kc * 128 + 128]
                        tx(lambda E, i=rc, o=out, kc=kc: E.matmul(
                            o, i, id_b[:], start=(kc == 0), stop=(kc == 3),
                            is_transpose=True),
                           reads=[Rrzt[qs], Rconst], writes=[Rb[7]],
                           inc=("T" if kc == 3 else None))
                    rzslot = h % 2
                    rzv = ah[0:1, RZ0 + rzslot * 512: RZ0 + rzslot * 512 + 512]
                    vx(lambda E, o=rzv: E.tensor_copy(o, pbb[7][0:1, 0:512]),
                       reads=[Rb[7]], writes=[Rrz[rzslot]], inc="V")
                    if hh % 2 == 1:
                        rzA = ah[0:1, RZ0: RZ0 + 512]
                        tx(lambda E, r=rzA: E.matmul(pb[7][0:64, :],
                                                     ones_b[0:1, 0:64], r,
                                                     start=True, stop=True),
                           reads=[Rrz[0], Rconst], writes=[Rb[7]], inc=None)
                        tx(lambda E, r=rzv: E.matmul(pb[7][64:128, :],
                                                     ones_b[0:1, 0:64], r,
                                                     start=True, stop=True),
                           reads=[Rrz[1], Rconst], writes=[Rb[7]], inc="T")
                        vx(lambda E: E.tensor_copy(csb[:], pb[7][:]),
                           reads=[Rb[7]], writes=[Rcsb], inc="V")
                        avA, avB = 3 + hh - 1, 3 + hh
                        gA = ah[0:64, GLO + fc * 512: GLO + fc * 512 + 512]
                        gB = ah[64:128, GLO + fc * 512: GLO + fc * 512 + 512]
                        vx(lambda E, o=gA, ab=avA: E.tensor_mul(
                            o, pb[ab][0:64, :], csb[0:64, :]),
                           reads=[Rb[avA], Rcsb], writes=[Rglo], inc="V")
                        vx(lambda E, o=gB, ab=avB: E.tensor_mul(
                            o, pb[ab][64:128, :], csb[64:128, :]),
                           reads=[Rb[avB], Rcsb], writes=[Rglo], inc="V")

            for m in range(NCH):
                bank = m % 3
                for k in range(16):
                    lhs = gw1a[:, k * 1024 + m * 128: k * 1024 + m * 128 + 128]
                    base = LOC if k < 8 else GLO
                    rr = Rloc if k < 8 else Rglo
                    rhs = ah[:, base + (k % 8) * 512: base + (k % 8) * 512 + 512]
                    tx(lambda E, l=lhs, r=rhs, b=bank, k=k: E.matmul(
                        pb[b][:], l, r, start=(k == 0), stop=(k == 15)),
                       reads=[Rgw1, rr], writes=[Rb[bank]],
                       inc=("T" if k == 15 else None))
                msl = ah[:, MIX + m * 512: MIX + m * 512 + 512]
                sx(lambda E, b=bank, o=msl, m=m: E.activation(
                    o, pb[b][:], AF.Sigmoid, bias=gb_s[:, m:m + 1]),
                   reads=[Rb[bank], Rgb], writes=[Rmix], inc="S")
            for fc in range(NCH):
                lsl = ah[:, LOC + fc * 512: LOC + fc * 512 + 512]
                gsl = ah[:, GLO + fc * 512: GLO + fc * 512 + 512]
                msl = ah[:, MIX + fc * 512: MIX + fc * 512 + 512]
                t1 = ah[:, PL0 + (fc % 4) * 512: PL0 + (fc % 4) * 512 + 512]
                vx(lambda E, a=lsl, b=gsl, o=t1: E.tensor_sub(o, a, b),
                   reads=[Rloc, Rglo], writes=[Rpl[fc % 4]], inc="V")
                vx(lambda E, a=t1, b=msl, o=t1: E.tensor_mul(o, a, b),
                   reads=[Rpl[fc % 4], Rmix], writes=[Rpl[fc % 4]], inc="V")
                vx(lambda E, a=t1, b=gsl, o=msl: E.tensor_add(o, a, b),
                   reads=[Rpl[fc % 4], Rglo], writes=[Rmix], inc="V")
            for m in range(NCH):
                bank = 3 + (m % 3)
                for k in range(NCH):
                    lhs = outwa[:, k * 1024 + m * 128: k * 1024 + m * 128 + 128]
                    rhs = ah[:, MIX + k * 512: MIX + k * 512 + 512]
                    tx(lambda E, l=lhs, r=rhs, b=bank, k=k: E.matmul(
                        pb[b][:], l, r, start=(k == 0), stop=(k == 7)),
                       reads=[Routw, Rmix], writes=[Rb[bank]],
                       inc=("T" if k == 7 else None))
                xsl = xy[:, m * 1024 + scb: m * 1024 + scb + 512]
                vx(lambda E, b=bank, o=xsl, m=m: E.scalar_tensor_tensor(
                    o, pb[b][:], ob_s[:, m:m + 1], o, ALU.add, ALU.add),
                   reads=[Rb[bank], Rob, Rxy], writes=[Rxy], inc="V")
            ln_stage(scb, "ln1", t, (6, 7, 0, 1), lambda fc: Rpl[fc % 2])

        alias_edge([Rah], Rpl + Rpg + Rrz + [Rloc, Rglo, Rmix])
        alias_edge(Rl2, [Rxpet])
        alias_edge(Rout, [Rxpef])
        if t == 0:
            dmaw("dgw1", lambda E: E.dma_start(
                out=gw1a.rearrange("p (k c) -> p k c", c=2048), in_=w1_src),
                 writes=[Rgw1])
            wb_store(gw1a.rearrange("p (k c) -> p k c", c=2048), w1b_src, Rgw1, Rsw1)
        else:
            P.emit("gpsimd", lambda E: E.dma_start(
                out=gw1a.rearrange("p (k c) -> p k c", c=2048), in_=w1b_src),
                reads=[Rsw1], writes=[Rgw1], inc="dgw1", k=16)
        if t == 0:
            alias_edge(Rw2, Rconv)

        for sc in range(2):
            scb = sc * 512
            bank_lg = 2
            for fc in range(NCH):
                lhs = rw_s[:, fc * 8: fc * 8 + 8]
                rhs = ynbf[:, fc * 1024 + scb: fc * 1024 + scb + 512]
                tx(lambda E, l=lhs, r=rhs, fc=fc: E.matmul(
                    pb[bank_lg][0:8, :], l, r, start=(fc == 0), stop=(fc == 7)),
                   reads=[Rrw, Rynbf], writes=[Rb[bank_lg]],
                   inc=("T" if fc == 7 else None))
            sx(lambda E: E.activation(ebf[0:8, :], pb[bank_lg][0:8, :], AF.Exp,
                                      bias=rb_s[:]),
               reads=[Rb[bank_lg], Rrb], writes=[Rebf], inc="S")
            bank_et = 3
            for u in range(4):
                tin = ebf[0:8, u * 128: u * 128 + 128]
                out = pb[bank_et][:, u * 8: u * 8 + 8]
                tx(lambda E, i=tin, o=out, u=u: E.matmul(
                    o, i, id_f[0:8, 0:8], start=(u == 0), stop=(u == 3),
                    is_transpose=True),
                   reads=[Rebf, Rconst], writes=[Rb[bank_et]],
                   inc=("T" if u == 3 else None))
            et = pb[bank_et][:, 0:32].rearrange("p (a b) -> p a b", b=8)
            zt = scr4[:, 8:12]
            rzt = scr4[:, 12:16]
            m1 = scr4[:, 0:4]
            m2 = scr4[:, 4:8]
            eq = scrf[:, 0:32]
            dd = scrf[:, 32:64]
            vx(lambda E: E.reduce_sum(zt, et, axis=AX.X), reads=[Rb[bank_et]],
               writes=[Rscr4], inc="V")
            vx(lambda E: E.reciprocal(rzt, zt), reads=[Rscr4], writes=[Rscr4], inc="V")
            vx(lambda E: E.reduce_max(m1, et, axis=AX.X), reads=[Rb[bank_et]],
               writes=[Rscr4], inc="V")
            m1b = m1.rearrange("p (a b) -> p a b", b=1).broadcast_to((128, 4, 8))
            vx(lambda E: E.tensor_tensor(eq.rearrange("p (a b) -> p a b", b=8), et, m1b,
                                         ALU.is_equal),
               reads=[Rb[bank_et], Rscr4], writes=[Rscrf], inc="V")
            vx(lambda E: E.scalar_tensor_tensor(dd, eq, -1e9, et.rearrange("p a b -> p (a b)"),
                                                ALU.mult, ALU.add),
               reads=[Rscrf, Rb[bank_et]], writes=[Rscrf], inc="V")
            vx(lambda E: E.reduce_max(m2, dd.rearrange("p (a b) -> p a b", b=8), axis=AX.X),
               reads=[Rscrf], writes=[Rscr4], inc="V")
            m2b = m2.rearrange("p (a b) -> p a b", b=1).broadcast_to((128, 4, 8))
            vx(lambda E: E.tensor_tensor(eq.rearrange("p (a b) -> p a b", b=8), et, m2b,
                                         ALU.is_ge),
               reads=[Rb[bank_et], Rscr4], writes=[Rscrf], inc="V")
            vx(lambda E: E.tensor_tensor(dd.rearrange("p (a b) -> p a b", b=8),
                                         eq.rearrange("p (a b) -> p a b", b=8), et,
                                         ALU.mult),
               reads=[Rscrf, Rb[bank_et]], writes=[Rscrf], inc="V")
            rztb = rzt.rearrange("p (a b) -> p a b", b=1).broadcast_to((128, 4, 8))
            vx(lambda E: E.tensor_tensor(ctm.rearrange("p (a b) -> p a b", b=8),
                                         dd.rearrange("p (a b) -> p a b", b=8), rztb,
                                         ALU.mult),
               reads=[Rscrf, Rscr4], writes=[Rctm], inc="V")
            bank_cf = 4
            for u in range(4):
                tin = ctm[:, u * 8: u * 8 + 8]
                out = pb[bank_cf][0:8, u * 128: u * 128 + 128]
                tx(lambda E, i=tin, o=out, u=u: E.matmul(
                    o, i, id_f[:], start=(u == 0), stop=(u == 3),
                    is_transpose=True),
                   reads=[Rctm, Rconst], writes=[Rb[bank_cf]],
                   inc=("T" if u == 3 else None))
            vx(lambda E: E.tensor_copy(cfm[0:8, :], pb[bank_cf][0:8, 0:512]),
               reads=[Rb[bank_cf]], writes=[Rcfm], inc="V")

            for er in range(16):
                e = er // 2
                mslot = e % 2
                msb = mwsb[:, mslot * 512: mslot * 512 + 512]
                if er % 2 == 0:
                    mwb_bank = 5 + mslot
                    lhs = basis[0:8, e * 128: e * 128 + 128]
                    tx(lambda E, l=lhs, b=mwb_bank: E.matmul(pb[b][:], l, cfm[0:8, :],
                                                             start=True, stop=True),
                       reads=[Rcfm, Rconst], writes=[Rb[mwb_bank]], inc="T")
                    sx(lambda E, b=mwb_bank, o=msb: E.copy(o, pb[b][:]),
                       reads=[Rb[mwb_bank]], writes=[Rmwsb[mslot]], inc="S")
                bank = er % 3
                for k in range(NCH):
                    lhs = gw1a[:, k * 2048 + er * 128: k * 2048 + er * 128 + 128]
                    rhs = ynbf[:, k * 1024 + scb: k * 1024 + scb + 512]
                    tx(lambda E, l=lhs, r=rhs, b=bank, k=k: E.matmul(
                        pb[b][:], l, r, start=(k == 0), stop=(k == 7)),
                       reads=[Rgw1, Rynbf], writes=[Rb[bank]],
                       inc=("T" if k == 7 else None))
                tsl = tbuf[:, er * 512: er * 512 + 512]
                vx(lambda E, b=bank, o=tsl, m=msb: E.tensor_mul(o, pb[b][:], m),
                   reads=[Rb[bank], Rmwsb[mslot]], writes=[Rtb], inc="V")

            for e8 in range(8):
                reg = e8 % 2
                dst = w2a[:, reg * 8192: reg * 8192 + 8192].rearrange(
                    "p (k c) -> p k c", c=512)
                if t == 0 and sc == 0:
                    dmaw("dw2a" if reg == 0 else "dw2b",
                         lambda E, d=dst, e8=e8: E.dma_start(out=d, in_=w2_src[:, e8]),
                         writes=[Rw2[reg]])
                    wb_store(dst, w2b_src[:, e8], Rw2[reg], Rsw2,
                             "dsw2" if reg == 0 else "dsw2b")
                else:
                    P.emit("gpsimd",
                           lambda E, d=dst, e8=e8: E.dma_start(out=d, in_=w2b_src[:, e8]),
                           reads=[Rsw2], writes=[Rw2[reg]],
                           inc=("dw2a" if reg == 0 else "dw2b"), k=16)
                for f4 in range(4):
                    f = e8 * 4 + f4
                    bank = f % 3
                    for k in range(16):
                        lhs = w2a[:, reg * 8192 + k * 512 + f4 * 128:
                                  reg * 8192 + k * 512 + f4 * 128 + 128]
                        rhs = tbuf[:, k * 512: k * 512 + 512]
                        tx(lambda E, l=lhs, r=rhs, b=bank, k=k: E.matmul(
                            pb[b][:], l, r, start=(k == 0), stop=(k == 15)),
                           reads=[Rw2[reg], Rtb], writes=[Rb[bank]],
                           inc=("T" if k == 15 else None))
                    hsl = ah[:, f * 512: f * 512 + 512]
                    sx(lambda E, b=bank, o=hsl: E.activation(o, pb[b][:], AF.Gelu),
                       reads=[Rb[bank]], writes=[Rah], inc="S")

            if t == 0 and sc == 0:
                Rsw2.writers = {"dsw2": P.cnt["dsw2"], "dsw2b": P.cnt["dsw2b"]}
            for kf in range(32):
                slot = (kf // 2) % 2
                if kf % 2 == 0:
                    dst = xpet[:, slot * 2048: slot * 2048 + 2048].rearrange(
                        "p (k c) -> p k c", c=1024)
                    if t == 0 and sc == 0:
                        dmaw("dl2a" if slot == 0 else "dl2b",
                             lambda E, d=dst, kf=kf: E.dma_start(
                                 out=d, in_=l2_src[:, kf:kf + 2, :]),
                             writes=[Rl2[slot]])
                        wb_store(dst, l2b_src[:, kf:kf + 2, :], Rl2[slot], Rsl2,
                                 "dsl2s" if slot == 0 else "dsl2t")
                    else:
                        P.emit("gpsimd",
                               lambda E, d=dst, kf=kf: E.dma_start(
                                   out=d, in_=l2b_src[:, kf:kf + 2, :]),
                               reads=[Rsl2], writes=[Rl2[slot]],
                               inc=("dl2a" if slot == 0 else "dl2b"), k=16)
                off = (kf % 2) * 1024
                for m in range(NCH):
                    lhs = xpet[:, slot * 2048 + off + m * 128:
                               slot * 2048 + off + m * 128 + 128]
                    rhs = ah[:, kf * 512: kf * 512 + 512]
                    tx(lambda E, l=lhs, r=rhs, b=m, kf=kf: E.matmul(
                        pb[b][:], l, r, start=(kf == 0), stop=(kf == 31)),
                       reads=[Rl2[slot], Rah], writes=[Rb[m]],
                       inc=("T" if (kf % 2 == 1 and m == NCH - 1) else None))
            if t == 0 and sc == 0:
                Rsl2.writers = {"dsl2s": P.cnt["dsl2s"], "dsl2t": P.cnt["dsl2t"]}
            for m in range(NCH):
                ysl = ynbf[:, m * 1024 + scb: m * 1024 + scb + 512]
                vx(lambda E, b=m, y=ysl, m=m: E.scalar_tensor_tensor(
                    y, pb[b][:], lb_s[:, m:m + 1], y, ALU.add, ALU.add),
                   reads=[Rb[m], Rlb, Rynbf], writes=[Rynbf], inc="V")
            ln_stage(scb, "ln2", t, (0, 1, 2, 3), lambda fc: Rah)

        if t + 1 < NT:
            P.emit("gpsimd", lambda E: E.dma_start(
                out=gw1a.rearrange("p (k c) -> p k c", c=1024), in_=gwb_src),
                reads=[Rsgw], writes=[Rgw1], inc="dgw1", k=16)
            P.emit("gpsimd", lambda E: E.dma_start(
                out=outwa.rearrange("p (k c) -> p k c", c=1024), in_=owb_src),
                reads=[Rsow], writes=[Routw], inc="doutw", k=16)

    P.raw_wait("sync", {"dout0": P.cnt["dout0"], "dout1": P.cnt["dout1"]})

    semh = {}
    for name in ("T", "V", "S", "G") + DMA_SEMS:
        semh[name] = stack.enter_context(nc.semaphore(name))

    ctx_lp = nc.allow_low_precision(
        reason="bf16 operands for matmuls are intended; fp32 PSUM accumulation")
    ctx_lp.__enter__()
    with nc.Block() as block:
        def run_queue(qname):
            def runner(E):
                for item in P.q[qname]:
                    if item[0] == "w":
                        E.wait_ge(semh[item[1]], item[2])
                    else:
                        _, fn, inc, k = item
                        ins = fn(E)
                        if inc is not None:
                            ins.then_inc(semh[inc], k)
            return runner

        block.tensor(run_queue("tensor"))
        block.vector(run_queue("vector"))
        block.scalar(run_queue("scalar"))
        block.sync(run_queue("sync"))
        block.gpsimd(run_queue("gpsimd"))

    ctx_lp.__exit__(None, None, None)
    stack.close()
    return nc


_NC_CACHE = None


def kernel(L0_attn_x, L0_attn_pos_emb, L0_attn_global_memory, L0_attn_conv_w,
           L0_attn_conv_b, L0_attn_gate_w, L0_attn_gate_b, L0_attn_out_w,
           L0_attn_out_b, L0_ffn_router_w, L0_ffn_router_b, L0_ffn_w1,
           L0_ffn_w2, L0_ffn_lin2_w, L0_ffn_lin2_b):
    global _NC_CACHE
    if _NC_CACHE is None:
        _NC_CACHE = build_nc()
    nc = _NC_CACHE

    f32 = np.float32
    ac = np.ascontiguousarray
    pos = ac(L0_attn_pos_emb[:S], dtype=f32)
    pos_f = ac(pos.T)
    convw = np.concatenate(
        [L0_attn_conv_w.transpose(2, 1, 0).reshape(4096, D),
         L0_attn_conv_b.reshape(1, D)], axis=0)
    convw = ac(convw, dtype=f32)
    gate_w = ac(L0_attn_gate_w, dtype=f32)
    out_w = ac(L0_attn_out_w, dtype=f32)
    w1fh = ac(L0_ffn_w1.transpose(1, 0, 2).reshape(D, 2048), dtype=f32)
    w2fh = ac(L0_ffn_w2.reshape(2048, FF), dtype=f32)
    lin2 = ac(L0_ffn_lin2_w, dtype=f32)
    gb = ac(L0_attn_gate_b.reshape(8, 128).T, dtype=f32)
    ob = ac(L0_attn_out_b.reshape(8, 128).T, dtype=f32)
    lb = ac(L0_ffn_lin2_b.reshape(8, 128).T, dtype=f32)
    rw = ac(L0_ffn_router_w.reshape(8, 128, 8).transpose(1, 0, 2).reshape(128, 64),
            dtype=f32)
    rb = ac(L0_ffn_router_b.reshape(8, 1), dtype=f32)
    gm = ac(L0_attn_global_memory[0], dtype=f32)

    in_maps = []
    for b in range(8):
        xb = ac(L0_attn_x[b], dtype=f32)
        in_maps.append({
            "x_tm": xb, "x_fm": ac(xb.T),
            "pos_tm": pos, "pos_fm": pos_f,
            "gmem": gm, "convw": convw,
            "gate_w": gate_w, "out_w": out_w,
            "w1f": w1fh, "w2f": w2fh, "lin2_w": lin2,
            "gate_b": gb, "out_b": ob, "lin2_b": lb,
            "router_w": rw, "router_b": rb,
        })
    res = run_bass_kernel_spmd(nc, in_maps, core_ids=list(range(8)))
    out = np.empty((8, S, D), np.float32)
    for b in range(8):
        out[b] = res.results[b]["out_fm"].T
    return out


# version 13
# speedup vs baseline: 1.3131x; 1.1114x over previous
import numpy as np
import concourse.bass as bass
import concourse.mybir as mybir
from concourse.bass_utils import run_bass_kernel_spmd

F32 = mybir.dt.float32
BF16 = mybir.dt.bfloat16
AF = mybir.ActivationFunctionType
ALU = mybir.AluOpType
AX = mybir.AxisListType

S, D, H, HD, WIN = 4096, 1024, 16, 64, 512
TILE, NT, SC = 1024, 4, 512
NCH = 8
E8, RR, FF = 8, 256, 4096
SCALE = 0.125
CEXP = -8.0

ENGINES = ("tensor", "vector", "scalar", "sync", "gpsimd")


class Reg:
    __slots__ = ("writers", "readers")

    def __init__(self):
        self.writers = {}
        self.readers = {}

    def wrote(self, gate):
        s, v = gate
        self.writers = {s: v}
        self.readers = {}

    def read(self, gate):
        s, v = gate
        if self.readers.get(s, 0) < v:
            self.readers[s] = v

def alias_edge(dsts, srcs):
    merged = {}
    for o in srcs:
        for d in (o.writers, o.readers):
            for s, v in d.items():
                if merged.get(s, 0) < v:
                    merged[s] = v
    for o in srcs:
        o.writers = {}
        o.readers = {}
    for r in dsts:
        r.writers = {}
        r.readers = dict(merged)


class Prog:
    def __init__(self):
        self.q = {e: [] for e in ENGINES}
        self.cnt = {}
        self.guar = {e: {} for e in ENGINES}
        self.pend = {e: [] for e in ENGINES}

    def sem(self, name):
        self.cnt[name] = 0
        return name

    def _wait(self, eng, sname, val):
        if val <= 0:
            return
        if self.guar[eng].get(sname, 0) >= val:
            return
        self.guar[eng][sname] = val
        self.q[eng].append(("w", sname, val))

    def emit(self, eng, fn, reads=(), writes=(), inc=None, k=1):
        need = {}
        for r in reads:
            for s, v in r.writers.items():
                if need.get(s, 0) < v:
                    need[s] = v
        for r in writes:
            for d in (r.writers, r.readers):
                for s, v in d.items():
                    if need.get(s, 0) < v:
                        need[s] = v
        if inc is not None and inc[0] == "d":
            need.pop(inc, None)
        for s, v in need.items():
            self._wait(eng, s, v)
        if inc is None:
            self.q[eng].append(("op", fn, None, 0))
            self.pend[eng].append((tuple(reads), tuple(writes)))
            return None
        self.cnt[inc] += k
        gate = (inc, self.cnt[inc])
        if eng == "tensor" and inc == "T":
            self.guar[eng][inc] = self.cnt[inc]
        self.q[eng].append(("op", fn, inc, k))
        for rr, ww in self.pend[eng]:
            for r in rr:
                r.read(gate)
            for w in ww:
                w.wrote(gate)
        self.pend[eng].clear()
        for r in reads:
            r.read(gate)
        for r in writes:
            r.wrote(gate)
        return gate

    def raw_wait(self, eng, gatedict):
        for s, v in gatedict.items():
            self._wait(eng, s, v)


def build_nc():
    nc = bass.Bass("TRN2", target_bir_lowering=False, debug=False)
    P = Prog()
    for s in ("T", "V", "S", "G"):
        P.sem(s)
    DMA_SEMS = ("dxy", "dsl0", "dsl1", "dpf0", "dpf1", "dgw1", "doutw",
                "dw2a", "dw2b", "dl2a", "dl2b", "dmisc", "dbias",
                "dout0", "dout1", "dsb", "dsw2", "dsw2b", "dsl2s", "dsl2t")
    for s in DMA_SEMS:
        P.sem(s)

    dt_in = lambda n, sh, dt=F32: nc.dram_tensor(n, sh, dt, kind="ExternalInput").ap()
    x_tm = dt_in("x_tm", [S, D])
    x_fm = dt_in("x_fm", [D, S])
    pos_tm = dt_in("pos_tm", [S, D])
    pos_fm = dt_in("pos_fm", [D, S])
    gmem = dt_in("gmem", [64, D])
    convw = dt_in("convw", [4097, D])
    gate_w = dt_in("gate_w", [2048, D])
    out_w = dt_in("out_w", [D, D])
    w1f = dt_in("w1f", [D, 2048])
    w2f = dt_in("w2f", [2048, FF])
    lin2_w = dt_in("lin2_w", [FF, D])
    gate_b = dt_in("gate_b", [128, 8])
    out_b = dt_in("out_b", [128, 8])
    lin2_b = dt_in("lin2_b", [128, 8])
    router_w = dt_in("router_w", [128, 64])
    router_b = dt_in("router_b", [8, 1])
    out_fm = nc.dram_tensor("out_fm", [D, S], F32, kind="ExternalOutput").ap()
    dt_sc = lambda n, sh: nc.dram_tensor(n, sh, BF16, kind="Internal").ap()
    gw_bf = dt_sc("gw_bf", [2048, D])
    ow_bf = dt_sc("ow_bf", [D, D])
    w1_bf = dt_sc("w1_bf", [D, 2048])
    w2_bf = dt_sc("w2_bf", [2048, FF])
    l2_bf = dt_sc("l2_bf", [FF, D])

    import contextlib
    stack = contextlib.ExitStack()
    sb = lambda n, sh, dt: stack.enter_context(nc.sbuf_tensor(n, sh, dt)).ap()
    ps = lambda n, sh, dt: stack.enter_context(nc.psum_tensor(n, sh, dt)).ap()

    xy = sb("xy", [128, 8192], BF16)
    ynbf = sb("ynbf", [128, 8192], BF16)
    xpef = sb("xpef", [128, 4096], BF16)
    xpet = sb("xpet", [128, 4096], BF16)
    ah = sb("ah", [128, 16384], BF16)
    tbuf = sb("tbuf", [128, 8192], BF16)
    gw1a = sb("gw1a", [128, 16384], BF16)
    outwa = sb("outwa", [128, 8192], BF16)
    w2a = sb("w2a", [128, 16384], BF16)
    gt_tm = sb("gt_tm", [128, 1024], BF16)

    gt_fm = sb("gt_fm", [128, 1024], BF16)
    ebf = sb("ebf", [128, 512], F32)
    cfm = sb("cfm", [128, 512], BF16)
    ctm = sb("ctm", [128, 32], F32)
    csb = sb("csb", [128, 512], BF16)
    mwsb = sb("mwsb", [128, 512], BF16)
    l2a = sb("l2a", [128, 4096], BF16)
    ztm = sb("ztm", [128, 32], F32)
    rztm = sb("rztm", [128, 32], BF16)
    scr4 = sb("scr4", [128, 16], F32)
    scrf = sb("scrf", [128, 640], F32)
    outst = sb("outst", [128, 512], F32)
    statv = sb("statv", [1, 1536], F32)
    ones_b = sb("ones_b", [128, 128], BF16)
    ones_f = sb("ones_f", [128, 128], F32)
    id_f = sb("id_f", [128, 128], F32)
    id_b = sb("id_b", [128, 128], BF16)
    basis = sb("basis", [8, 1024], BF16)
    neg8 = sb("neg8", [128, 1], F32)
    epsc = sb("epsc", [128, 1], F32)
    gb_s = sb("gb_s", [128, 8], F32)
    ob_s = sb("ob_s", [128, 8], F32)
    lb_s = sb("lb_s", [128, 8], F32)
    rb_s = sb("rb_s", [8, 1], F32)
    rw_s = sb("rw_s", [128, 64], BF16)
    iot = sb("iot", [128, 1], F32)

    cbrow = l2a[0:1, 0:1024]
    pb = [ps(f"pb{i}", [128, 512], F32) for i in range(8)]
    pbb = [p.bitcast(BF16) for p in pb]

    LOC, GLO, MIX, PL0, PG0, RZ0 = 0, 4096, 8192, 12288, 14336, 15360

    R = lambda: Reg()
    Rxy, Rynbf, Rxpef, Rxpet = R(), R(), R(), R()
    Rloc, Rglo, Rmix = R(), R(), R()
    Rpl = [R() for _ in range(4)]
    Rpg = [R() for _ in range(2)]
    Rrz = [R() for _ in range(2)]
    Rah = R()
    Rtb = R()
    Rslab = [R() for _ in range(6)]
    Rgw1, Routw = R(), R()
    Rw2 = [R() for _ in range(2)]
    Rconv = [R() for _ in range(4)]
    Rl2 = [R() for _ in range(4)]
    Rout = [R() for _ in range(2)]
    Rgt_tm, Rgt_fm = R(), R()
    Rgb, Rob, Rlb, Rrb, Rrw, Rcbrow = R(), R(), R(), R(), R(), R()
    Rsgw, Rsow, Rsw1, Rsw2, Rsl2 = R(), R(), R(), R(), R()
    Rebf, Rcfm, Rctm, Rscr4, Rscrf, Rstat = R(), R(), R(), R(), R(), R()
    Rzt = [R(), R()]
    Rrzt = [R(), R()]
    Rcsb = R()
    Rmwsb = [R(), R()]
    Rb = [R() for _ in range(8)]
    Rconst = R()
    Rbias = R()


    def gp(fn, **kw):
        return P.emit("gpsimd", fn, **kw)

    def vx(fn, **kw):
        return P.emit("vector", fn, **kw)

    def sx(fn, **kw):
        return P.emit("scalar", fn, **kw)

    def tx(fn, **kw):
        return P.emit("tensor", fn, **kw)

    def dmaw(sem, fn, **kw):
        return P.emit("gpsimd", fn, inc=sem, k=16, **kw)

    def dmax(sem, fn, **kw):
        return P.emit("sync", fn, inc=sem, k=16, **kw)

    gp(lambda E: E.memset(ones_b[:], 1.0), writes=[Rconst], inc="G")
    gp(lambda E: E.memset(ones_f[:], 1.0), writes=[Rconst], inc="G")
    gp(lambda E: E.memset(neg8[:], CEXP), writes=[Rconst], inc="G")
    gp(lambda E: E.memset(epsc[:], 1e-5), writes=[Rconst], inc="G")
    gp(lambda E: E.iota(id_f[:], pattern=[[1, 128]], channel_multiplier=0,
                        allow_small_or_imprecise_dtypes=True),
       writes=[Rconst], inc="G")
    gp(lambda E: E.iota(iot[:], pattern=[[0, 1]], channel_multiplier=1,
                        allow_small_or_imprecise_dtypes=True),
       writes=[Rconst], inc="G")
    vx(lambda E: E.tensor_scalar(id_f[:], id_f[:], iot[:], None, ALU.is_equal),
       reads=[Rconst], writes=[Rconst], inc="V")
    vx(lambda E: E.tensor_copy(id_b[:], id_f[:]), reads=[Rconst], writes=[Rconst], inc="V")
    gp(lambda E: E.iota(basis[:], pattern=[[1, 8], [0, 128]], channel_multiplier=0,
                        allow_small_or_imprecise_dtypes=True),
       writes=[Rconst], inc="G")
    vx(lambda E: E.tensor_scalar(basis[:], basis[:], iot[0:8, :], None, ALU.is_equal),
       reads=[Rconst], writes=[Rconst], inc="V")
    dmax("dbias", lambda E: E.dma_start(out=gb_s[:], in_=gate_b), writes=[Rgb])
    dmax("dbias", lambda E: E.dma_start(out=ob_s[:], in_=out_b), writes=[Rob])
    dmax("dbias", lambda E: E.dma_start(out=lb_s[:], in_=lin2_b), writes=[Rlb])
    dmax("dbias", lambda E: E.dma_start(out=rb_s[:], in_=router_b), writes=[Rrb])
    dmaw("dmisc", lambda E: E.dma_start(out=rw_s[:], in_=router_w), writes=[Rrw])
    dmaw("dmisc", lambda E: E.dma_start(out=gt_tm[64:128, :], in_=gmem), writes=[Rgt_tm])
    dmaw("dmisc", lambda E: E.dma_start(out=cbrow[:], in_=convw[4096:4097, :]), writes=[Rcbrow])
    conv_src = convw[0:4096, :].rearrange("(j p) c -> p j c", p=128)

    def conv_group_load(g):
        slot = g % 2
        dst = w2a[:, slot * 4096:(slot + 1) * 4096].rearrange("p (j c) -> p j c", c=1024)
        dmaw("dw2a" if slot == 0 else "dw2b",
             lambda E, g=g, d=dst: E.dma_start(out=d, in_=conv_src[:, 4 * g:4 * g + 4, :]),
             writes=[Rconv[slot]])

    for r in (Rgb, Rob, Rlb, Rrb):
        r.writers = {"dbias": P.cnt["dbias"]}
    for r in (Rrw, Rgt_tm, Rcbrow):
        r.writers = {"dmisc": P.cnt["dmisc"]}

    conv_group_load(0)
    conv_group_load(1)
    gw_src = gate_w.rearrange("(k p) c -> p k c", p=128)
    w1_src = w1f.rearrange("(k p) c -> p k c", p=128)
    l2_src = lin2_w.rearrange("(k p) c -> p k c", p=128)
    w2_src = w2f.rearrange("(k p) (e c) -> p e k c", p=128, c=512)
    gwb_src = gw_bf.rearrange("(k p) c -> p k c", p=128)
    owb_src = ow_bf.rearrange("(k p) c -> p k c", p=128)
    w1b_src = w1_bf.rearrange("(k p) c -> p k c", p=128)
    l2b_src = l2_bf.rearrange("(k p) c -> p k c", p=128)
    w2b_src = w2_bf.rearrange("(k p) (e c) -> p e k c", p=128, c=512)

    def wb_store(sbuf_view, dst_view, slab_reg, scr_reg, sem="dsb"):
        P.emit("sync", lambda E, s=sbuf_view, d=dst_view: E.dma_start(out=d, in_=s),
               reads=[slab_reg], writes=[scr_reg], inc=sem, k=16)

    dmaw("dgw1", lambda E: E.dma_start(out=gw1a.rearrange("p (k c) -> p k c", c=1024),
                               in_=gw_src),
         writes=[Rgw1])
    wb_store(gw1a.rearrange("p (k c) -> p k c", c=1024), gwb_src, Rgw1, Rsgw)
    ow_src = out_w.rearrange("(k p) c -> p k c", p=128)
    dmaw("doutw", lambda E: E.dma_start(out=outwa.rearrange("p (k c) -> p k c", c=1024),
                                in_=ow_src),
         writes=[Routw])
    wb_store(outwa.rearrange("p (k c) -> p k c", c=1024), owb_src, Routw, Rsow)
    for r in (Rsgw, Rsow):
        r.writers = {"dsb": P.cnt["dsb"]}
    Rgw1.readers["dsb"] = P.cnt["dsb"]
    Routw.readers["dsb"] = P.cnt["dsb"]


    def ln_stage(sc_base, out_kind, t, bank_ids, scratch_reg):
        bsy, bsq, bb0, bb1 = bank_ids
        srct, srcr = (xy, Rxy) if out_kind == "ln1" else (ynbf, Rynbf)
        for fc in range(NCH):
            xsl = srct[:, fc * 1024 + sc_base: fc * 1024 + sc_base + 512]
            pl = ah[:, PL0 + (fc % 2) * 512: PL0 + (fc % 2) * 512 + 512]
            srg = scratch_reg(fc)
            sx(lambda E, i=xsl, o=pl: E.activation(o, i, AF.Square),
               reads=[srcr], writes=[srg], inc="S")
            tx(lambda E, r=xsl, fc=fc: E.matmul(pb[bsy][0:1, :], ones_b[:, 0:1], r,
                                                start=(fc == 0), stop=(fc == 7)),
               reads=[srcr, Rconst], writes=[Rb[bsy]], inc="T")
            tx(lambda E, r=pl, fc=fc: E.matmul(pb[bsq][0:1, :], ones_b[:, 0:1], r,
                                               start=(fc == 0), stop=(fc == 7)),
               reads=[srg, Rconst], writes=[Rb[bsq]], inc="T")
        mean = statv[0:1, 0:512]
        inv = statv[0:1, 512:1024]
        vx(lambda E: E.tensor_scalar(mean, pb[bsy][0:1, :], 1.0 / 1024, None, ALU.mult),
           reads=[Rb[bsy]], writes=[Rstat], inc="V")
        vx(lambda E: E.tensor_mul(inv, mean, mean), reads=[Rstat], writes=[Rstat],
           inc="V")
        vx(lambda E: E.scalar_tensor_tensor(inv, pb[bsq][0:1, :], 1.0 / 1024, inv,
                                            ALU.mult, ALU.subtract),
           reads=[Rb[bsq], Rstat], writes=[Rstat], inc="V")
        sx(lambda E: E.activation(inv, inv, AF.Ln, bias=epsc[0:1, :]),
           reads=[Rstat, Rconst], writes=[Rstat], inc="S")
        sx(lambda E: E.activation(inv, inv, AF.Exp, scale=-0.5),
           reads=[Rstat], writes=[Rstat], inc="S")
        invb = statv.bitcast(BF16)[0:1, 2048:2560]
        minvb = statv.bitcast(BF16)[0:1, 2560:3072]
        vx(lambda E: E.tensor_copy(invb, inv), reads=[Rstat], writes=[Rstat], inc="V")
        vx(lambda E: E.tensor_mul(mean, mean, inv), reads=[Rstat], writes=[Rstat],
           inc="V")
        vx(lambda E: E.tensor_copy(minvb, mean), reads=[Rstat], writes=[Rstat],
           inc="V")
        tx(lambda E: E.matmul(pb[bb0][:], ones_b[0:1, :], invb, start=True, stop=True),
           reads=[Rstat, Rconst], writes=[Rb[bb0]], inc="T")
        tx(lambda E: E.matmul(pb[bb1][:], ones_b[0:1, :], minvb, start=True, stop=True),
           reads=[Rstat, Rconst], writes=[Rb[bb1]], inc="T")
        for fc in range(NCH):
            xsl = srct[:, fc * 1024 + sc_base: fc * 1024 + sc_base + 512]
            q1 = scrf[:, 0:512]
            vx(lambda E, i=xsl, o=q1: E.tensor_mul(o, i, pb[bb0][:]),
               reads=[srcr, Rb[bb0]], writes=[Rscrf], inc="V")
            if out_kind == "ln1":
                osl = ynbf[:, fc * 1024 + sc_base: fc * 1024 + sc_base + 512]
                vx(lambda E, i=q1, o=osl: E.tensor_sub(o, i, pb[bb1][:]),
                   reads=[Rscrf, Rb[bb1]], writes=[Rynbf], inc="V")
            else:
                slot = 0
                osl = outst[:, 0:512]
                vx(lambda E, i=q1, o=osl: E.tensor_sub(o, i, pb[bb1][:]),
                   reads=[Rscrf, Rb[bb1]], writes=[Rout[slot]], inc="V")
                dst = out_fm[fc * 128:(fc + 1) * 128,
                             t * 1024 + sc_base: t * 1024 + sc_base + 512]
                P.emit("sync", lambda E, s=osl, d=dst: E.dma_start(out=d, in_=s),
                       reads=[Rout[slot]], inc="dout%d" % slot, k=16)

    for t in range(NT):
        tb0 = t * 1024
        if t > 0:
            alias_edge(Rpl + Rpg + Rrz + [Rloc, Rglo, Rmix], [Rah])
        xfm_g = x_fm.rearrange("(f p) s -> p f s", p=128)
        pfm_g = pos_fm.rearrange("(f p) s -> p f s", p=128)
        xtm_g = x_tm.rearrange("(g p) d -> p g d", p=128)
        ptm_g = pos_tm.rearrange("(g p) d -> p g d", p=128)
        def emit_A(tt_, sc_):
            base = tt_ * 1024 + sc_ * 512
            if tt_ >= NT:
                return
            dmaw("dpf0", lambda E, s=xfm_g[:, :, base: base + 512],
                 d=xpef.rearrange("p (f s) -> p f s", s=512): E.dma_start(
                     out=d, in_=s),
                 writes=[Rxpef])
            dmaw("dpf1", lambda E, s=pfm_g[:, :, base: base + 512],
                 d=xpef.rearrange("p (f s) -> p f s", s=512): E.dma_start(
                     out=d, in_=s, accum_op=ALU.add),
                 writes=[Rxpef])
            g0 = base // 128
            dmaw("dsl0", lambda E, s=xtm_g[:, g0:g0 + 4, :],
                 d=xpet.rearrange("p (g d) -> p g d", d=1024): E.dma_start(
                     out=d, in_=s),
                 writes=[Rxpet])
            dmaw("dsl1", lambda E, s=ptm_g[:, g0:g0 + 4, :],
                 d=xpet.rearrange("p (g d) -> p g d", d=1024): E.dma_start(
                     out=d, in_=s, accum_op=ALU.add),
                 writes=[Rxpet])

        for sc in range(2):
            scb = sc * 512
            if t == 0 and sc == 0:
                emit_A(0, 0)
            if sc == 0:
                for fcx in range(NCH):
                    srcx = x_fm[fcx * 128:(fcx + 1) * 128, tb0:tb0 + 1024]
                    dstx = xy[:, fcx * 1024: fcx * 1024 + 1024]
                    dmaw("dxy", lambda E, s=srcx, d=dstx: E.dma_start(out=d, in_=s),
                         writes=[Rxy])

            scnt = 0
            pcnt = 0
            for q in range(4):
                qs = q % 2
                for hp in range(2):
                    hA = q * 4 + hp * 2
                    fc = hA // 2
                    pair0 = pcnt
                    def emit_av(hh2, pbase, kc):
                        h = q * 4 + hh2
                        av_bank = 3 + hh2
                        slot = (pair0 + kc * 2 + (hh2 - hp * 2)) % 4
                        pls = ah[:, PL0 + slot * 512: PL0 + slot * 512 + 512]
                        lhs = xpet[:, kc * 1024 + 64 * h:
                                   kc * 1024 + 64 * h + 64]
                        out = pb[av_bank][pbase:pbase + 64, :]
                        tx(lambda E, l=lhs, r=pls, o=out, kc=kc: E.matmul(
                            o, l, r, start=(kc == 0), stop=(kc == 3)),
                           reads=[Rpl[slot], Rxpet], writes=[Rb[av_bank]],
                           inc=("T" if kc == 3 else None))

                    for kc in range(4):
                        for hh2, pbase in ((hp * 2, 0), (hp * 2 + 1, 64)):
                            h = q * 4 + hh2
                            b = scnt % 3
                            scnt += 1
                            lhs = xpef[pbase:pbase + 64,
                                       fc * 512 + kc * 128:
                                       fc * 512 + kc * 128 + 128]
                            rhs = xpef[pbase:pbase + 64,
                                       fc * 512: fc * 512 + 512]
                            tx(lambda E, l=lhs, r=rhs, b=b: E.matmul(
                                pb[b][:], l, r, start=True, stop=True),
                               reads=[Rxpef], writes=[Rb[b]], inc="T")
                            slot = pcnt % 4
                            pcnt += 1
                            pls = ah[:, PL0 + slot * 512: PL0 + slot * 512 + 512]
                            zc = ztm[:, qs * 16 + hh2 * 4 + kc:
                                     qs * 16 + hh2 * 4 + kc + 1]
                            sx(lambda E, b=b, o=pls, z=zc: E.activation(
                                o, pb[b][:], AF.Exp, bias=neg8[:], scale=SCALE,
                                accum_out=z),
                               reads=[Rb[b], Rconst], writes=[Rpl[slot], Rzt[qs]],
                               inc="S")
                        if kc >= 1:
                            emit_av(hp * 2, 0, kc - 1)
                            emit_av(hp * 2 + 1, 64, kc - 1)
                    emit_av(hp * 2, 0, 3)
                    emit_av(hp * 2 + 1, 64, 3)
                vx(lambda E, qs=qs: E.reciprocal(rztm[:, qs * 16: qs * 16 + 16],
                                                 ztm[:, qs * 16: qs * 16 + 16]),
                   reads=[Rzt[qs]], writes=[Rrzt[qs]], inc="V")
                for hh in range(4):
                    h = q * 4 + hh
                    fc, pbase = h // 2, 64 * (h % 2)
                    hbase = (h % 2) * 64
                    for kc in range(4):
                        rc = rztm[:, qs * 16 + hh * 4 + kc:
                                  qs * 16 + hh * 4 + kc + 1].broadcast_to((128, 64))
                        out = pb[7][hbase:hbase + 64, kc * 128:kc * 128 + 128]
                        tx(lambda E, i=rc, o=out, kc=kc: E.matmul(
                            o, i, id_b[:], start=(kc == 0), stop=(kc == 3)),
                           reads=[Rrzt[qs], Rconst], writes=[Rb[7]],
                           inc=("T" if kc == 3 else None))
                    if hh % 2 == 1:
                        vx(lambda E: E.tensor_copy(csb[:], pb[7][:]),
                           reads=[Rb[7]], writes=[Rcsb], inc="V")
                        avA, avB = 3 + hh - 1, 3 + hh
                        lA = ah[0:64, LOC + fc * 512: LOC + fc * 512 + 512]
                        lB = ah[64:128, LOC + fc * 512: LOC + fc * 512 + 512]
                        vx(lambda E, o=lA, ab=avA: E.tensor_mul(
                            o, pb[ab][0:64, :], csb[0:64, :]),
                           reads=[Rb[avA], Rcsb], writes=[Rloc], inc="V")
                        vx(lambda E, o=lB, ab=avB: E.tensor_mul(
                            o, pb[ab][64:128, :], csb[64:128, :]),
                           reads=[Rb[avB], Rcsb], writes=[Rloc], inc="V")

            if t == 0 and sc == 0:
                for j in range(32):
                    kc, ic = j // 8, j % 8
                    g, jj = j // 4, j % 4
                    slot = g % 2
                    if jj == 0 and g >= 2:
                        conv_group_load(g)
                    lhs = xpef[:, ic * 512: ic * 512 + 256].rearrange(
                        "p (t f) -> p t f", f=4)[:, :, kc]
                    for oh in range(2):
                        bank = 3 + oh
                        rhs = w2a[:, slot * 4096 + jj * 1024 + oh * 512:
                                  slot * 4096 + jj * 1024 + oh * 512 + 512]
                        tx(lambda E, l=lhs, r=rhs, b=bank, j=j: E.matmul(
                            pb[b][0:64, :], l, r, start=(j == 0), stop=False),
                           reads=[Rxpef, Rconv[slot]], writes=[Rb[bank]],
                           inc=("T" if (j % 4) == 3 and oh == 1 else None))
                for oh in range(2):
                    bank = 3 + oh
                    rhsb = cbrow[0:1, oh * 512: oh * 512 + 512]
                    tx(lambda E, r=rhsb, b=bank: E.matmul(
                        pb[b][0:64, :], ones_b[0:1, 0:64], r, start=False, stop=True),
                       reads=[Rcbrow, Rconst], writes=[Rb[bank]], inc="T")
                    vx(lambda E, b=bank, oh=oh: E.tensor_copy(
                        gt_tm[0:64, oh * 512: oh * 512 + 512], pb[b][0:64, :]),
                       reads=[Rb[bank]], writes=[Rgt_tm], inc="V")
                for fc in range(NCH):
                    bank = 5
                    tin = gt_tm[:, fc * 128: fc * 128 + 128]
                    tx(lambda E, i=tin, b=bank: E.matmul(
                        pbb[b][:, 0:128], i, id_b[:], start=True, stop=True,
                        is_transpose=True),
                       reads=[Rgt_tm, Rconst], writes=[Rb[bank]], inc="T")
                    vx(lambda E, fc=fc, b=bank: E.tensor_copy(
                        gt_fm[:, fc * 128: fc * 128 + 128], pbb[b][:, 0:128]),
                       reads=[Rb[bank]], writes=[Rgt_fm], inc="V")

            scnt = 0
            for q in range(4):
                qs = q % 2
                for hh in range(4):
                    h = q * 4 + hh
                    fc, pbase = h // 2, 64 * (h % 2)
                    b = scnt % 2
                    scnt += 1
                    lhs = gt_fm[pbase:pbase + 64, fc * 128: fc * 128 + 128]
                    rhs = xpef[pbase:pbase + 64, fc * 512: fc * 512 + 512]
                    tx(lambda E, l=lhs, r=rhs, b=b: E.matmul(
                        pb[b][:], l, r, start=True, stop=True),
                       reads=[Rgt_fm, Rxpef], writes=[Rb[b]], inc="T")
                    slot = h % 2
                    pgs = ah[:, PG0 + slot * 512: PG0 + slot * 512 + 512]
                    sx(lambda E, b=b, o=pgs: E.activation(o, pb[b][:], AF.Exp,
                                                          bias=neg8[:], scale=SCALE),
                       reads=[Rb[b], Rconst], writes=[Rpg[slot]], inc="S")
                    for tt in range(4):
                        zout = pb[2][:, qs * 16 + hh * 4 + tt: qs * 16 + hh * 4 + tt + 1]
                        tx(lambda E, l=pgs, o=zout, tt=tt, hh=hh, qs=qs: E.matmul(
                            o, l[:, tt * 128:tt * 128 + 128], ones_b[:, 0:1],
                            start=(hh == 0 and tt == 0), stop=(hh == 3 and tt == 3)),
                           reads=[Rpg[slot], Rconst], writes=[Rb[2]],
                           inc=("T" if (hh == 3 and tt == 3) else None))
                    av_bank = 3 + hh
                    lhs2 = gt_tm[:, h * 64: h * 64 + 64]
                    out = pb[av_bank][pbase:pbase + 64, :]
                    tx(lambda E, l=lhs2, r=pgs, o=out: E.matmul(
                        o, l, r, start=True, stop=True),
                       reads=[Rpg[slot], Rgt_tm], writes=[Rb[av_bank]], inc="T")
                vx(lambda E, qs=qs: E.reciprocal(rztm[:, qs * 16: qs * 16 + 16],
                                                 pb[2][:, qs * 16: qs * 16 + 16]),
                   reads=[Rb[2]], writes=[Rrzt[qs]], inc="V")
                for hh in range(4):
                    h = q * 4 + hh
                    fc, pbase = h // 2, 64 * (h % 2)
                    hbase = (h % 2) * 64
                    for kc in range(4):
                        rc = rztm[:, qs * 16 + hh * 4 + kc:
                                  qs * 16 + hh * 4 + kc + 1].broadcast_to((128, 64))
                        out = pb[7][hbase:hbase + 64, kc * 128:kc * 128 + 128]
                        tx(lambda E, i=rc, o=out, kc=kc: E.matmul(
                            o, i, id_b[:], start=(kc == 0), stop=(kc == 3)),
                           reads=[Rrzt[qs], Rconst], writes=[Rb[7]],
                           inc=("T" if kc == 3 else None))
                    if hh % 2 == 1:
                        vx(lambda E: E.tensor_copy(csb[:], pb[7][:]),
                           reads=[Rb[7]], writes=[Rcsb], inc="V")
                        avA, avB = 3 + hh - 1, 3 + hh
                        gA = ah[0:64, GLO + fc * 512: GLO + fc * 512 + 512]
                        gB = ah[64:128, GLO + fc * 512: GLO + fc * 512 + 512]
                        vx(lambda E, o=gA, ab=avA: E.tensor_mul(
                            o, pb[ab][0:64, :], csb[0:64, :]),
                           reads=[Rb[avA], Rcsb], writes=[Rglo], inc="V")
                        vx(lambda E, o=gB, ab=avB: E.tensor_mul(
                            o, pb[ab][64:128, :], csb[64:128, :]),
                           reads=[Rb[avB], Rcsb], writes=[Rglo], inc="V")

            emit_A(t + (sc == 1), (sc + 1) % 2)

            for m in range(NCH):
                bank = m % 3
                for k in range(16):
                    lhs = gw1a[:, k * 1024 + m * 128: k * 1024 + m * 128 + 128]
                    base = LOC if k < 8 else GLO
                    rr = Rloc if k < 8 else Rglo
                    rhs = ah[:, base + (k % 8) * 512: base + (k % 8) * 512 + 512]
                    tx(lambda E, l=lhs, r=rhs, b=bank, k=k: E.matmul(
                        pb[b][:], l, r, start=(k == 0), stop=(k == 15)),
                       reads=[Rgw1, rr], writes=[Rb[bank]],
                       inc=("T" if k == 15 else None))
                msl = ah[:, MIX + m * 512: MIX + m * 512 + 512]
                sx(lambda E, b=bank, o=msl, m=m: E.activation(
                    o, pb[b][:], AF.Sigmoid, bias=gb_s[:, m:m + 1]),
                   reads=[Rb[bank], Rgb], writes=[Rmix], inc="S")
            for fc in range(NCH):
                lsl = ah[:, LOC + fc * 512: LOC + fc * 512 + 512]
                gsl = ah[:, GLO + fc * 512: GLO + fc * 512 + 512]
                msl = ah[:, MIX + fc * 512: MIX + fc * 512 + 512]
                t1 = ah[:, PL0 + (fc % 4) * 512: PL0 + (fc % 4) * 512 + 512]
                vx(lambda E, a=lsl, b=gsl, o=t1: E.tensor_sub(o, a, b),
                   reads=[Rloc, Rglo], writes=[Rpl[fc % 4]], inc="V")
                vx(lambda E, a=t1, b=msl, o=t1: E.tensor_mul(o, a, b),
                   reads=[Rpl[fc % 4], Rmix], writes=[Rpl[fc % 4]], inc="V")
                vx(lambda E, a=t1, b=gsl, o=msl: E.tensor_add(o, a, b),
                   reads=[Rpl[fc % 4], Rglo], writes=[Rmix], inc="V")
            for m in range(NCH):
                bank = 3 + (m % 3)
                for k in range(NCH):
                    lhs = outwa[:, k * 1024 + m * 128: k * 1024 + m * 128 + 128]
                    rhs = ah[:, MIX + k * 512: MIX + k * 512 + 512]
                    tx(lambda E, l=lhs, r=rhs, b=bank, k=k: E.matmul(
                        pb[b][:], l, r, start=(k == 0), stop=(k == 7)),
                       reads=[Routw, Rmix], writes=[Rb[bank]],
                       inc=("T" if k == 7 else None))
                xsl = xy[:, m * 1024 + scb: m * 1024 + scb + 512]
                vx(lambda E, b=bank, o=xsl, m=m: E.scalar_tensor_tensor(
                    o, pb[b][:], ob_s[:, m:m + 1], o, ALU.add, ALU.add),
                   reads=[Rb[bank], Rob, Rxy], writes=[Rxy], inc="V")
            ln_stage(scb, "ln1", t, (6, 7, 0, 1), lambda fc: Rpl[fc % 2])

        alias_edge([Rah], Rpl + Rpg + Rrz + [Rloc, Rglo, Rmix])
        if t == 0:
            dmaw("dgw1", lambda E: E.dma_start(
                out=gw1a.rearrange("p (k c) -> p k c", c=2048), in_=w1_src),
                 writes=[Rgw1])
            wb_store(gw1a.rearrange("p (k c) -> p k c", c=2048), w1b_src, Rgw1, Rsw1)
        else:
            P.emit("gpsimd", lambda E: E.dma_start(
                out=gw1a.rearrange("p (k c) -> p k c", c=2048), in_=w1b_src),
                reads=[Rsw1], writes=[Rgw1], inc="dgw1", k=16)
        if t == 0:
            alias_edge(Rw2, Rconv)

        for sc in range(2):
            scb = sc * 512
            bank_lg = 2
            for fc in range(NCH):
                lhs = rw_s[:, fc * 8: fc * 8 + 8]
                rhs = ynbf[:, fc * 1024 + scb: fc * 1024 + scb + 512]
                tx(lambda E, l=lhs, r=rhs, fc=fc: E.matmul(
                    pb[bank_lg][0:8, :], l, r, start=(fc == 0), stop=(fc == 7)),
                   reads=[Rrw, Rynbf], writes=[Rb[bank_lg]],
                   inc=("T" if fc == 7 else None))
            sx(lambda E: E.activation(ebf[0:8, :], pb[bank_lg][0:8, :], AF.Exp,
                                      bias=rb_s[:]),
               reads=[Rb[bank_lg], Rrb], writes=[Rebf], inc="S")
            bank_et = 3
            for u in range(4):
                tin = ebf[0:8, u * 128: u * 128 + 128]
                out = pb[bank_et][:, u * 8: u * 8 + 8]
                tx(lambda E, i=tin, o=out, u=u: E.matmul(
                    o, i, id_f[0:8, 0:8], start=(u == 0), stop=(u == 3),
                    is_transpose=True),
                   reads=[Rebf, Rconst], writes=[Rb[bank_et]],
                   inc=("T" if u == 3 else None))
            et = pb[bank_et][:, 0:32].rearrange("p (a b) -> p a b", b=8)
            zt = scr4[:, 8:12]
            rzt = scr4[:, 12:16]
            m1 = scr4[:, 0:4]
            m2 = scr4[:, 4:8]
            eq = scrf[:, 512:544]
            dd = scrf[:, 544:576]
            vx(lambda E: E.reduce_sum(zt, et, axis=AX.X), reads=[Rb[bank_et]],
               writes=[Rscr4], inc="V")
            vx(lambda E: E.reciprocal(rzt, zt), reads=[Rscr4], writes=[Rscr4], inc="V")
            vx(lambda E: E.reduce_max(m1, et, axis=AX.X), reads=[Rb[bank_et]],
               writes=[Rscr4], inc="V")
            m1b = m1.rearrange("p (a b) -> p a b", b=1).broadcast_to((128, 4, 8))
            vx(lambda E: E.tensor_tensor(eq.rearrange("p (a b) -> p a b", b=8), et, m1b,
                                         ALU.is_equal),
               reads=[Rb[bank_et], Rscr4], writes=[Rscrf], inc="V")
            vx(lambda E: E.scalar_tensor_tensor(dd, eq, -1e9, et.rearrange("p a b -> p (a b)"),
                                                ALU.mult, ALU.add),
               reads=[Rscrf, Rb[bank_et]], writes=[Rscrf], inc="V")
            vx(lambda E: E.reduce_max(m2, dd.rearrange("p (a b) -> p a b", b=8), axis=AX.X),
               reads=[Rscrf], writes=[Rscr4], inc="V")
            m2b = m2.rearrange("p (a b) -> p a b", b=1).broadcast_to((128, 4, 8))
            vx(lambda E: E.tensor_tensor(eq.rearrange("p (a b) -> p a b", b=8), et, m2b,
                                         ALU.is_ge),
               reads=[Rb[bank_et], Rscr4], writes=[Rscrf], inc="V")
            vx(lambda E: E.tensor_tensor(dd.rearrange("p (a b) -> p a b", b=8),
                                         eq.rearrange("p (a b) -> p a b", b=8), et,
                                         ALU.mult),
               reads=[Rscrf, Rb[bank_et]], writes=[Rscrf], inc="V")
            rztb = rzt.rearrange("p (a b) -> p a b", b=1).broadcast_to((128, 4, 8))
            vx(lambda E: E.tensor_tensor(ctm.rearrange("p (a b) -> p a b", b=8),
                                         dd.rearrange("p (a b) -> p a b", b=8), rztb,
                                         ALU.mult),
               reads=[Rscrf, Rscr4], writes=[Rctm], inc="V")
            bank_cf = 4
            for u in range(4):
                tin = ctm[:, u * 8: u * 8 + 8]
                out = pb[bank_cf][0:8, u * 128: u * 128 + 128]
                tx(lambda E, i=tin, o=out, u=u: E.matmul(
                    o, i, id_f[:], start=(u == 0), stop=(u == 3),
                    is_transpose=True),
                   reads=[Rctm, Rconst], writes=[Rb[bank_cf]],
                   inc=("T" if u == 3 else None))
            vx(lambda E: E.tensor_copy(cfm[0:8, :], pb[bank_cf][0:8, 0:512]),
               reads=[Rb[bank_cf]], writes=[Rcfm], inc="V")

            for er in range(16):
                e = er // 2
                mslot = 0
                msb = mwsb[:, 0:512]
                if er % 2 == 0:
                    mwb_bank = 5 + mslot
                    lhs = basis[0:8, e * 128: e * 128 + 128]
                    tx(lambda E, l=lhs, b=mwb_bank: E.matmul(pb[b][:], l, cfm[0:8, :],
                                                             start=True, stop=True),
                       reads=[Rcfm, Rconst], writes=[Rb[mwb_bank]], inc="T")
                    sx(lambda E, b=mwb_bank, o=msb: E.copy(o, pb[b][:]),
                       reads=[Rb[mwb_bank]], writes=[Rmwsb[mslot]], inc="S")
                bank = er % 3
                for k in range(NCH):
                    lhs = gw1a[:, k * 2048 + er * 128: k * 2048 + er * 128 + 128]
                    rhs = ynbf[:, k * 1024 + scb: k * 1024 + scb + 512]
                    tx(lambda E, l=lhs, r=rhs, b=bank, k=k: E.matmul(
                        pb[b][:], l, r, start=(k == 0), stop=(k == 7)),
                       reads=[Rgw1, Rynbf], writes=[Rb[bank]],
                       inc=("T" if k == 7 else None))
                tsl = tbuf[:, er * 512: er * 512 + 512]
                vx(lambda E, b=bank, o=tsl, m=msb: E.tensor_mul(o, pb[b][:], m),
                   reads=[Rb[bank], Rmwsb[mslot]], writes=[Rtb], inc="V")

            for e8 in range(8):
                reg = e8 % 2
                dst = w2a[:, reg * 8192: reg * 8192 + 8192].rearrange(
                    "p (k c) -> p k c", c=512)
                if t == 0 and sc == 0:
                    dmaw("dw2a" if reg == 0 else "dw2b",
                         lambda E, d=dst, e8=e8: E.dma_start(out=d, in_=w2_src[:, e8]),
                         writes=[Rw2[reg]])
                    wb_store(dst, w2b_src[:, e8], Rw2[reg], Rsw2,
                             "dsw2" if reg == 0 else "dsw2b")
                else:
                    P.emit("gpsimd",
                           lambda E, d=dst, e8=e8: E.dma_start(out=d, in_=w2b_src[:, e8]),
                           reads=[Rsw2], writes=[Rw2[reg]],
                           inc=("dw2a" if reg == 0 else "dw2b"), k=16)
                for f4 in range(4):
                    f = e8 * 4 + f4
                    bank = f % 3
                    for k in range(16):
                        lhs = w2a[:, reg * 8192 + k * 512 + f4 * 128:
                                  reg * 8192 + k * 512 + f4 * 128 + 128]
                        rhs = tbuf[:, k * 512: k * 512 + 512]
                        tx(lambda E, l=lhs, r=rhs, b=bank, k=k: E.matmul(
                            pb[b][:], l, r, start=(k == 0), stop=(k == 15)),
                           reads=[Rw2[reg], Rtb], writes=[Rb[bank]],
                           inc=("T" if k == 15 else None))
                    hsl = ah[:, f * 512: f * 512 + 512]
                    sx(lambda E, b=bank, o=hsl: E.activation(o, pb[b][:], AF.Gelu),
                       reads=[Rb[bank]], writes=[Rah], inc="S")

            if t == 0 and sc == 0:
                Rsw2.writers = {"dsw2": P.cnt["dsw2"], "dsw2b": P.cnt["dsw2b"]}
            if t == 0 and sc == 0:
                alias_edge([Rl2[0]], [Rcbrow])
            for kf in range(32):
                slot = (kf // 2) % 2
                if kf % 2 == 0:
                    dst = l2a[:, slot * 2048: slot * 2048 + 2048].rearrange(
                        "p (k c) -> p k c", c=1024)
                    if t == 0 and sc == 0:
                        dmaw("dl2a" if slot == 0 else "dl2b",
                             lambda E, d=dst, kf=kf: E.dma_start(
                                 out=d, in_=l2_src[:, kf:kf + 2, :]),
                             writes=[Rl2[slot]])
                        wb_store(dst, l2b_src[:, kf:kf + 2, :], Rl2[slot], Rsl2,
                                 "dsl2s" if slot == 0 else "dsl2t")
                    else:
                        P.emit("gpsimd",
                               lambda E, d=dst, kf=kf: E.dma_start(
                                   out=d, in_=l2b_src[:, kf:kf + 2, :]),
                               reads=[Rsl2], writes=[Rl2[slot]],
                               inc=("dl2a" if slot == 0 else "dl2b"), k=16)
                off = (kf % 2) * 1024
                for m in range(NCH):
                    lhs = l2a[:, slot * 2048 + off + m * 128:
                              slot * 2048 + off + m * 128 + 128]
                    rhs = ah[:, kf * 512: kf * 512 + 512]
                    tx(lambda E, l=lhs, r=rhs, b=m, kf=kf: E.matmul(
                        pb[b][:], l, r, start=(kf == 0), stop=(kf == 31)),
                       reads=[Rl2[slot], Rah], writes=[Rb[m]],
                       inc=("T" if (kf % 2 == 1 and m == NCH - 1) else None))
            if t == 0 and sc == 0:
                Rsl2.writers = {"dsl2s": P.cnt["dsl2s"], "dsl2t": P.cnt["dsl2t"]}
            for m in range(NCH):
                ysl = ynbf[:, m * 1024 + scb: m * 1024 + scb + 512]
                vx(lambda E, b=m, y=ysl, m=m: E.scalar_tensor_tensor(
                    y, pb[b][:], lb_s[:, m:m + 1], y, ALU.add, ALU.add),
                   reads=[Rb[m], Rlb, Rynbf], writes=[Rynbf], inc="V")
            ln_stage(scb, "ln2", t, (0, 1, 2, 3), lambda fc: Rah)

        if t + 1 < NT:
            P.emit("gpsimd", lambda E: E.dma_start(
                out=gw1a.rearrange("p (k c) -> p k c", c=1024), in_=gwb_src),
                reads=[Rsgw], writes=[Rgw1], inc="dgw1", k=16)
            P.emit("gpsimd", lambda E: E.dma_start(
                out=outwa.rearrange("p (k c) -> p k c", c=1024), in_=owb_src),
                reads=[Rsow], writes=[Routw], inc="doutw", k=16)

    P.raw_wait("sync", {"dout0": P.cnt["dout0"], "dout1": P.cnt["dout1"]})

    semh = {}
    for name in ("T", "V", "S", "G") + DMA_SEMS:
        semh[name] = stack.enter_context(nc.semaphore(name))

    ctx_lp = nc.allow_low_precision(
        reason="bf16 operands for matmuls are intended; fp32 PSUM accumulation")
    ctx_lp.__enter__()
    with nc.Block() as block:
        def run_queue(qname):
            def runner(E):
                for item in P.q[qname]:
                    if item[0] == "w":
                        E.wait_ge(semh[item[1]], item[2])
                    else:
                        _, fn, inc, k = item
                        ins = fn(E)
                        if inc is not None:
                            ins.then_inc(semh[inc], k)
            return runner

        block.tensor(run_queue("tensor"))
        block.vector(run_queue("vector"))
        block.scalar(run_queue("scalar"))
        block.sync(run_queue("sync"))
        block.gpsimd(run_queue("gpsimd"))

    ctx_lp.__exit__(None, None, None)
    stack.close()
    return nc


_NC_CACHE = None


def kernel(L0_attn_x, L0_attn_pos_emb, L0_attn_global_memory, L0_attn_conv_w,
           L0_attn_conv_b, L0_attn_gate_w, L0_attn_gate_b, L0_attn_out_w,
           L0_attn_out_b, L0_ffn_router_w, L0_ffn_router_b, L0_ffn_w1,
           L0_ffn_w2, L0_ffn_lin2_w, L0_ffn_lin2_b):
    global _NC_CACHE
    if _NC_CACHE is None:
        _NC_CACHE = build_nc()
    nc = _NC_CACHE

    f32 = np.float32
    ac = np.ascontiguousarray
    pos = ac(L0_attn_pos_emb[:S], dtype=f32)
    pos_f = ac(pos.T)
    convw = np.concatenate(
        [L0_attn_conv_w.transpose(2, 1, 0).reshape(4096, D),
         L0_attn_conv_b.reshape(1, D)], axis=0)
    convw = ac(convw, dtype=f32)
    gate_w = ac(L0_attn_gate_w, dtype=f32)
    out_w = ac(L0_attn_out_w, dtype=f32)
    w1fh = ac(L0_ffn_w1.transpose(1, 0, 2).reshape(D, 2048), dtype=f32)
    w2fh = ac(L0_ffn_w2.reshape(2048, FF), dtype=f32)
    lin2 = ac(L0_ffn_lin2_w, dtype=f32)
    gb = ac(L0_attn_gate_b.reshape(8, 128).T, dtype=f32)
    ob = ac(L0_attn_out_b.reshape(8, 128).T, dtype=f32)
    lb = ac(L0_ffn_lin2_b.reshape(8, 128).T, dtype=f32)
    rw = ac(L0_ffn_router_w.reshape(8, 128, 8).transpose(1, 0, 2).reshape(128, 64),
            dtype=f32)
    rb = ac(L0_ffn_router_b.reshape(8, 1), dtype=f32)
    gm = ac(L0_attn_global_memory[0], dtype=f32)

    in_maps = []
    for b in range(8):
        xb = ac(L0_attn_x[b], dtype=f32)
        in_maps.append({
            "x_tm": xb, "x_fm": ac(xb.T),
            "pos_tm": pos, "pos_fm": pos_f,
            "gmem": gm, "convw": convw,
            "gate_w": gate_w, "out_w": out_w,
            "w1f": w1fh, "w2f": w2fh, "lin2_w": lin2,
            "gate_b": gb, "out_b": ob, "lin2_b": lb,
            "router_w": rw, "router_b": rb,
        })
    res = run_bass_kernel_spmd(nc, in_maps, core_ids=list(range(8)))
    out = np.empty((8, S, D), np.float32)
    for b in range(8):
        out[b] = res.results[b]["out_fm"].T
    return out
